# version 15 (final)
# baseline (speedup 1.0000x reference)
import contextlib
import numpy as np

import concourse.bass as bass
import concourse.mybir as mybir
from concourse.bass_utils import run_bass_kernel_spmd

F32 = mybir.dt.float32
BF16 = mybir.dt.bfloat16
AF = mybir.ActivationFunctionType
ALU = mybir.AluOpType
AX = mybir.AxisListType

D = 1024
DI = 2048
NS = 16
FF = 4096
NE = 8
KD = D // 128
KI = DI // 128
KF = FF // 128
LN_EPS = 1e-5
K_CLAMP = 60.0
TC = 512


class Sem:
    __slots__ = ("h", "v")

    def __init__(self, h):
        self.h = h
        self.v = 0

    def bump(self, k=1):
        self.v += k
        return self.v


class KB:

    def __init__(self, T=4096, n_phase=99):
        self.T = T
        self.NCH = T // TC
        self.n_phase = n_phase
        self.nc = bass.Bass("TRN2", target_bir_lowering=False, debug=False)
        self._semn = 0
        self._pools = {}
        self._cursor = {"c": 0, "hw": 0, "sw": 0}

    def sem(self, name=None):
        self._semn += 1
        h = self.st.enter_context(self.nc.semaphore(name or f"s{self._semn}"))
        return Sem(h)

    def begin_phase(self):
        self._cursor = {"c": 0, "hw": 0, "sw": 0}

    def gsem(self, kind="c", name=None):
        cur = self._cursor[kind]
        pool = self._pools.setdefault(kind, [])
        if cur == len(pool):
            pool.append(Sem(self.st.enter_context(
                self.nc.semaphore(f"{kind}{cur}"))))
        self._cursor[kind] += 1
        return pool[cur]

    def sb(self, name, shape, dt=F32, stack=None):
        return (stack or self.st).enter_context(
            self.nc.sbuf_tensor(name, list(shape), dt))

    def ps(self, name, shape, stack=None):
        return (stack or self.st).enter_context(
            self.nc.psum_tensor(name, list(shape), F32))

    def sec(self, **fns):
        order = ("sync", "tensor", "vector", "scalar", "gpsimd")
        for name in order:
            if name in fns:
                getattr(self.block, name)(fns[name])

    def barrier(self):
        ph = self.ph
        target = ph.v + 5

        def mk(e):
            def f(eng):
                if e in ("vector", "scalar", "gpsimd"):
                    eng.drain().then_inc(ph.h, 1)
                else:
                    eng.nop().then_inc(ph.h, 1)
                eng.wait_ge(ph.h, target)
            return f
        self.sec(**{e: mk(e) for e in
                    ("sync", "tensor", "vector", "scalar", "gpsimd")})
        ph.v = target

    def build(self):
        nc = self.nc
        T = self.T

        def din(name, shape, dt=F32):
            return nc.dram_tensor(name, list(shape), dt, kind="ExternalInput").ap()

        def dint(name, shape, dt=F32):
            return nc.dram_tensor(name, list(shape), dt, kind="Internal").ap()

        self.xT = din("xT", [D, T])
        self.xpT = din("xpT", [D, DI]); self.dtpT = din("dtpT", [D, DI])
        self.bcT = din("bcT", [D, 2 * NS]); self.opT = din("opT", [DI, D])
        self.A_log_pp = din("A_log_pp", [128, KI * NS])
        self.dtb_pp = din("dtb_pp", [128, KI])
        self.convw_pp = din("convw_pp", [128, KI * 4])
        self.convb_pp = din("convb_pp", [128, KI])
        self.D_pp = din("D_pp", [128, KI])
        self.WkT = din("WkT", [D, D]); self.WvT = din("WvT", [D, D])
        self.WrT = din("WrT", [D, D]); self.WoT = din("WoT", [D, D])
        self.FkT = din("FkT", [D, FF]); self.FvT = din("FvT", [FF, D])
        self.FrT = din("FrT", [D, D])
        for n in ("ln1g", "ln1b", "ln2g", "ln2b", "tmix", "cmix", "ftmix",
                  "fcmix", "tdec", "tfir", "mixb", "g1", "be1", "g2", "be2"):
            setattr(self, n + "_d", din(n + "_pp", [128, KD]))
        self.mixw = din("mixw", [D, D])
        self.routerw = din("routerw", [D, NE])
        self.w1 = din("w1", [NE, D, FF])
        self.b1_pp = din("b1_pp", [128, NE * KF])
        self.w2 = din("w2", [NE, FF, D])
        self.b2 = din("b2", [NE, D])
        self.eye32 = din("eye32", [128, 128])
        self.ones1 = din("ones1", [1, 128])
        self.ones1b = din("ones1b", [128, 1], BF16)
        self.sel16 = din("sel16", [NS, NS * 128], BF16)
        self.sel8 = din("sel8", [NE, NE * 128])

        self.yout = nc.dram_tensor("yout", [D, T], F32, kind="ExternalOutput").ap()
        self.x_work = dint("x_work", [D, T])
        self.s_dt = dint("s_dt", [DI, T], BF16)
        self.s_xc = dint("s_xc", [DI, T], BF16)
        self.s_bc = dint("s_bc", [2 * NS, T], BF16)
        self.s_n2 = dint("s_n2", [D, T], BF16)
        self.s_u2 = dint("s_u2", [D, T])
        self.s_ya = dint("s_ya", [D, T])
        self.s_cb = dint("s_cb", [NE, T], BF16)

        self.st = contextlib.ExitStack()
        with self.st:
            self._persistent()
            self.block = self.st.enter_context(nc.Block())
            self.ph = self.sem("ph")
            self.p0()
            self.barrier()
            if self.n_phase >= 1:
                self.mamba1()
                self.barrier()
            if self.n_phase >= 2:
                self.mamba2(src=self.xT, dst=self.x_work)
                self.barrier()
            if self.n_phase >= 3:
                self.ffn_pre(src=self.x_work, call=0)
                self.barrier()
            if self.n_phase >= 4:
                last = (self.n_phase == 4)
                self.ffn_exp(dst=(self.yout if last else self.x_work), call=0)
                self.barrier()
            if self.n_phase >= 5:
                self.rwkv1(src=self.x_work)
                self.barrier()
            if self.n_phase >= 6:
                self.rwkv2(src=self.x_work, dst=self.x_work)
                self.barrier()
            if self.n_phase >= 7:
                self.ffn_pre(src=self.x_work, call=1)
                self.barrier()
            if self.n_phase >= 8:
                self.ffn_exp(dst=self.yout, call=1)
                self.barrier()
            if self.n_phase in (2, 3, 5, 6, 7):
                eps = self.sem("ep")

                def f_ep(sync):
                    sync.dma_start(out=self.yout, in_=self.x_work).then_inc(eps.h, 16)
                    sync.wait_ge(eps.h, 16)
                eps.bump(16)
                self.sec(sync=f_ep)
            elif self.n_phase < 2:
                eps = self.sem("ep")

                def f_ep(sync):
                    sync.dma_start(out=self.yout, in_=self.xT).then_inc(eps.h, 16)
                    sync.wait_ge(eps.h, 16)
                self.sec(sync=f_ep)
        return nc

    def _persistent(self):
        sb = self.sb
        ppcols = [
            ("A", KI * NS), ("dtb", KI), ("convw", KI * 4), ("convb", KI),
            ("Dm", KI), ("ln1g", KD), ("ln1b", KD), ("ln2g", KD), ("ln2b", KD),
            ("tmix", KD), ("tmix1m", KD), ("cmix", KD),
            ("ftmix", KD), ("ftmix1m", KD), ("fcmix", KD),
            ("ed", KD), ("eu", KD),
            ("mixb", KD), ("g1", KD), ("be1", KD), ("g2", KD), ("be2", KD),
            ("b1", NE * KF), ("c_m1", 1), ("c_mh", 1),
        ]
        self.PP = {}
        off = 0
        for n, w in ppcols:
            self.PP[n] = (off, w)
            off += w
        self.pp = sb("pp", [128, off])
        self.ones_sb = sb("ones_sb", [1, 128])
        self.onesb_sb = sb("onesb_sb", [128, 1], BF16)
        self.mst = sb("mst", [128, KI * NS])
        self.wkvA = sb("wkvA", [128, KD])
        self.wkvB = sb("wkvB", [128, KD])

    def ppc(self, name, i=0):
        o, w = self.PP[name]
        return self.pp[:, o + i:o + i + 1]

    def ppr(self, name):
        o, w = self.PP[name]
        return self.pp[:, o:o + w]

    def p0(self):
        p0 = self.sem("p0")
        pairs = [
            (self.ppr("A"), self.A_log_pp), (self.ppr("dtb"), self.dtb_pp),
            (self.ppr("convw"), self.convw_pp),
            (self.ppr("convb"), self.convb_pp), (self.ppr("Dm"), self.D_pp),
            (self.ppr("mixb"), self.mixb_d), (self.ppr("b1"), self.b1_pp),
            (self.ones_sb[:], self.ones1), (self.onesb_sb[:], self.ones1b),
        ]
        for n in ("ln1g", "ln1b", "ln2g", "ln2b", "tmix", "cmix", "ftmix",
                  "fcmix", "g1", "be1", "g2", "be2"):
            pairs.append((self.ppr(n), getattr(self, n + "_d")))
        pairs.append((self.ppr("ed"), self.tdec_d))
        pairs.append((self.ppr("eu"), self.tfir_d))

        def f_sync(sync):
            for dst, src in pairs:
                sync.dma_start(out=dst, in_=src).then_inc(p0.h, 16)
        p0.bump(16 * len(pairs))

        ps0 = self.sem("p0s")
        pv0 = self.sem("p0v")

        def f_vec0(vector):
            vector.memset(self.ppr("c_m1"), -1.0)
            vector.memset(self.ppr("c_mh"), -0.5)
            vector.drain().then_inc(pv0.h, 1)
        pv0.bump()

        def f_scalar(scalar):
            scalar.wait_ge(p0.h, p0.v)
            scalar.wait_ge(pv0.h, pv0.v)
            scalar.activation(self.ppr("A"), self.ppr("A"), AF.Exp)
            scalar.drain()
            scalar.activation(self.ppr("ed"), self.ppr("ed"), AF.Exp)
            scalar.drain()
            scalar.activation(self.ppr("ed"), self.ppr("ed"), AF.Exp,
                              scale=self.ppc("c_m1"))
            scalar.drain()
            scalar.activation(self.ppr("eu"), self.ppr("eu"), AF.Exp)
            scalar.drain().then_inc(ps0.h, 1)
        ps0.bump()

        def f_vector(vector):
            vector.wait_ge(ps0.h, ps0.v)
            vector.tensor_scalar_mul(self.ppr("A"), self.ppr("A"), -1.0)
            vector.drain()
            vector.tensor_scalar(self.ppr("tmix1m"), self.ppr("tmix"), -1.0,
                                 1.0, ALU.mult, ALU.add)
            vector.drain()
            vector.tensor_scalar(self.ppr("ftmix1m"), self.ppr("ftmix"), -1.0,
                                 1.0, ALU.mult, ALU.add)
            vector.drain()
            vector.memset(self.mst[:], 0.0)
            vector.memset(self.wkvA[:], 0.0)
            vector.memset(self.wkvB[:], 0.0)
            vector.drain()
        self.sec(sync=f_sync, vector=f_vec0)
        self.sec(scalar=f_scalar)
        self.sec(vector=f_vector)

    def emit_ln(self, bufs, x3, n, out_f32=None, out_b16=None, g=None, b=None,
                wait_sv=0):
        sv, ss, sp = bufs["sv"], bufs["ss"], bufs["sp"]
        xb, sqb = bufs["xb"], bufs["sqb"]
        rows_sb, lnt = bufs["rows_sb"], bufs["lnt"]
        rows_ps, bc_ps = bufs["rows_ps"], bufs["bc_ps"]

        def fv1(vector):
            vector.tensor_copy(xb, x3).then_inc(sv.h, 1)
        v_xb = sv.bump()

        def fs1(scalar):
            if wait_sv:
                scalar.wait_ge(sv.h, wait_sv)
            scalar.activation(sqb, x3, AF.Square).then_inc(ss.h, 1)
        s_sq = ss.bump()

        def ft1(tensor):
            tensor.wait_ge(sv.h, v_xb)
            for kt in range(KD):
                mm = tensor.matmul(rows_ps[:, 0:n], self.onesb_sb[:],
                                   xb[:, kt, :], start=(kt == 0),
                                   stop=(kt == KD - 1))
            mm.then_inc(sp.h, 1)
            tensor.wait_ge(ss.h, s_sq)
            for kt in range(KD):
                mm = tensor.matmul(rows_ps[:, n:2 * n], self.onesb_sb[:],
                                   sqb[:, kt, :], start=(kt == 0),
                                   stop=(kt == KD - 1))
            mm.then_inc(sp.h, 1)
        p_rows = sp.bump(2)

        def fv2(vector):
            vector.wait_ge(sp.h, p_rows)
            vector.tensor_scalar_mul(rows_sb[:, 0:2 * n], rows_ps[:, 0:2 * n],
                                     1.0 / D)
            vector.drain()
            vector.tensor_mul(rows_sb[:, 2 * n:3 * n], rows_sb[:, 0:n],
                              rows_sb[:, 0:n])
            vector.drain()
            vector.tensor_sub(rows_sb[:, n:2 * n], rows_sb[:, n:2 * n],
                              rows_sb[:, 2 * n:3 * n])
            vector.drain()
            vector.tensor_scalar_add(rows_sb[:, n:2 * n], rows_sb[:, n:2 * n],
                                     LN_EPS).then_inc(sv.h, 1)
        v_var = sv.bump()

        def fs2(scalar):
            scalar.wait_ge(sv.h, v_var)
            scalar.activation(rows_sb[:, n:2 * n], rows_sb[:, n:2 * n], AF.Ln)
            scalar.drain()
            scalar.activation(rows_sb[:, n:2 * n], rows_sb[:, n:2 * n], AF.Exp,
                              scale=self.ppc("c_mh")[:1, :])
            scalar.drain().then_inc(ss.h, 1)
        s_rstd = ss.bump()

        def fv3(vector):
            vector.wait_ge(ss.h, s_rstd)
            vector.tensor_mul(rows_sb[:, 0:n], rows_sb[:, 0:n],
                              rows_sb[:, n:2 * n]).then_inc(sv.h, 1)
        v_mrs = sv.bump()

        def ft2(tensor):
            tensor.wait_ge(sv.h, v_mrs)
            mm = tensor.matmul(bc_ps[:, 0, :n], self.ones_sb[:],
                               rows_sb[:, n:2 * n], start=True, stop=True)
            mm.then_inc(sp.h, 1)
            mm = tensor.matmul(bc_ps[:, 1, :n], self.ones_sb[:],
                               rows_sb[:, 0:n], start=True, stop=True)
            mm.then_inc(sp.h, 1)
        p_bc = sp.bump(2)

        def fv4(vector):
            vector.wait_ge(sp.h, p_bc)
            kd = x3.shape[1]
            rb = bc_ps[:, 0:1, :n].broadcast_to((128, kd, n))
            mb = bc_ps[:, 1:2, :n].broadcast_to((128, kd, n))
            vector.tensor_mul(lnt, x3, rb)
            vector.drain()
            vector.tensor_sub(lnt, lnt, mb)
            vector.drain()
            if g is not None:
                for kt in range(KD):
                    vector.tensor_scalar(lnt[:, kt, :], lnt[:, kt, :],
                                         g[:, kt:kt + 1], b[:, kt:kt + 1],
                                         ALU.mult, ALU.add)
                vector.drain()
            if out_f32 is not None:
                vector.tensor_copy(out_f32, lnt)
                vector.drain()
            if out_b16 is not None:
                vector.tensor_copy(out_b16, lnt)
                vector.drain()
            vector.drain().then_inc(sv.h, 1)
        v_done = sv.bump()

        self.sec(vector=fv1, scalar=fs1)
        self.sec(tensor=ft1)
        self.sec(vector=fv2)
        self.sec(scalar=fs2)
        self.sec(vector=fv3)
        self.sec(tensor=ft2)
        self.sec(vector=fv4)
        return v_done

    def _halo_bufs(self, lnb):
        return {
            "xb": lnb["xb"][:, :, 0:1], "sqb": lnb["sqb"][:, :, 0:1],
            "rows_sb": lnb["rows_sb"][:, 0:3], "lnt": lnb["lnt"][:, :, 0:1],
            "rows_ps": lnb["rows_ps"][:, 0:2], "bc_ps": lnb["bc_ps"][:, :, 0:1],
            "sv": lnb["sv"], "ss": lnb["ss"], "sp": lnb["sp"],
        }

    def mk_ln_bufs(self, pst, tag, n):
        return {
            "xb": self.sb(tag + "xb", [128, KD * n], BF16, pst)[:]
                .rearrange("p (k t) -> p k t", k=KD),
            "sqb": self.sb(tag + "sq", [128, KD * n], BF16, pst)[:]
                .rearrange("p (k t) -> p k t", k=KD),
            "rows_sb": self.sb(tag + "rw", [1, 3 * n], F32, pst)[:],
            "lnt": self.sb(tag + "ln", [128, KD * n], F32, pst)[:]
                .rearrange("p (k t) -> p k t", k=KD),
            "rows_ps": self.ps(tag + "rp", [1, 2 * n], pst)[:],
            "bc_ps": self.ps(tag + "bp", [128, 2, n], pst)[:],
            "sv": self.gsem("c"), "ss": self.gsem("c"),
            "sp": self.gsem("c"),
        }

    def mamba1(self):
        self.begin_phase()
        NCH = self.NCH
        with contextlib.ExitStack() as pst:
            sbp = lambda n, s, d=F32: self.sb("m1" + n, s, d, pst)
            xsb = [sbp("x0", [128, KD * TC])]
            lnb = self.mk_ln_bufs(pst, "m1l", TC)
            xn_b = sbp("xn", [128, KD * TC], BF16)
            xpT_sb = sbp("wxp", [128, KD * DI], BF16)
            dtpT_sb = sbp("wdt", [128, KD * DI], BF16)
            bcT_sb = sbp("wbc", [128, KD * 2 * NS], BF16)
            xi = [sbp("xi0", [128, KI * (TC + 3)], BF16)]
            dt_st = [sbp("dt0", [128, KI * TC], BF16)]
            xc_st = [sbp("xc0", [128, KI * TC], BF16)]
            bc_st = [sbp("bc0", [2 * NS, TC], BF16)]
            ctmp_b = sbp("ctb", [128, KI * TC], BF16)

            proj_ps = self.ps("m1pp", [128, 2, TC], pst)
            bcp_ps = self.ps("m1cp", [2 * NS, TC], pst)

            xl = [self.gsem("hw")]
            wl = self.gsem("sw")
            spj = self.gsem("c")
            sevx = self.gsem("c")
            sevd = self.gsem("c")
            svdt = self.gsem("c")
            sact = self.gsem("c")
            scv = self.gsem("c")
            sxc = self.gsem("c")
            stl = self.gsem("c")
            sst = [self.gsem("hw")]
            sbc = self.gsem("c")
            sst_base0 = sst[0].v
            stl_base0 = stl.v

            def f_w(gpsimd):
                gpsimd.wait_ge(self.ph.h, self.ph.v)
                for dst, src, kk in ((xpT_sb, self.xpT, DI),
                                     (dtpT_sb, self.dtpT, DI),
                                     (bcT_sb, self.bcT, 2 * NS)):
                    gpsimd.dma_start(
                        out=dst[:].rearrange("p (kt m) -> p kt m", kt=KD),
                        in_=src.rearrange("(kt p) m -> p kt m", p=128),
                    ).then_inc(wl.h, 16)
            wl.bump(48)
            self.sec(gpsimd=f_w)

            xTv = self.xT.rearrange("(kt p) t -> p kt t", p=128)
            sdtv = self.s_dt.rearrange("(kt p) t -> p kt t", p=128)
            sxcv = self.s_xc.rearrange("(kt p) t -> p kt t", p=128)

            ln_done = {}
            scv_g0 = scv.v
            for c in range(NCH):
                sl = 0
                t0 = c * TC

                def f_ld(sync, t0=t0, c=c):
                    if c >= 1:
                        sync.wait_ge(lnb["sv"].h, ln_done[c - 1])
                        sync.wait_ge(scv.h, scv_g0 + 8 * c)
                    sync.dma_start(
                        out=xsb[0][:].rearrange("p (k t) -> p k t", k=KD),
                        in_=xTv[:, :, t0:t0 + TC]).then_inc(xl[0].h, 16)
                v_x = xl[0].bump(16)
                self.sec(sync=f_ld)

                spj_prev = spj.v

                def f_gate(eng, v=v_x, c=c, spj_prev=spj_prev):
                    eng.wait_ge(xl[0].h, v)
                    if c >= 1:
                        eng.wait_ge(spj.h, spj_prev)
                        eng.wait_ge(sst[0].h, sst_base0 + 48 * c)
                    eng.wait_ge(wl.h, wl.v)
                self.sec(tensor=f_gate, vector=f_gate, scalar=f_gate)

                x3 = xsb[0][:].rearrange("p (k t) -> p k t", k=KD)
                ln_done[c] = self.emit_ln(
                    lnb, x3, TC,
                    out_b16=xn_b[:].rearrange("p (k t) -> p k t", k=KD))

                xnv = xn_b[:].rearrange("p (k t) -> p k t", k=KD)
                xiv = xi[0][:].rearrange("p (k t) -> p k t", k=KI)

                def f_tail(vector, c=c):
                    if c == 0:
                        vector.memset(xiv[:, :, 0:3], 0.0)
                    else:
                        vector.tensor_copy(xiv[:, :, 0:3], xiv[:, :, TC:TC + 3])
                    vector.drain().then_inc(stl.h, 1)
                stl.bump(1)
                self.sec(vector=f_tail)

                spj_base = spj.v
                sevx_base = sevx.v
                sevd_base = sevd.v
                v_ln = ln_done[c]

                def f_proj(tensor, c=c, v_ln=v_ln, spj_base=spj_base,
                           sevx_base=sevx_base, sevd_base=sevd_base):
                    tensor.wait_ge(lnb["sv"].h, v_ln)
                    g = 0
                    for wsb in (xpT_sb, dtpT_sb):
                        wv = wsb[:].rearrange("p (kt m) -> p kt m", kt=KD)
                        for mt in range(KI):
                            bank = g % 2
                            if 2 <= g <= KI + 1:
                                tensor.wait_ge(sevx.h, sevx_base + g - 1)
                            elif g > KI + 1:
                                tensor.wait_ge(sevd.h, sevd_base + g - KI - 1)
                            for kt in range(KD):
                                mm = tensor.matmul(
                                    proj_ps[:, bank, :],
                                    wv[:, kt, mt * 128:(mt + 1) * 128],
                                    xnv[:, kt, :],
                                    start=(kt == 0), stop=(kt == KD - 1))
                            mm.then_inc(spj.h, 1)
                            g += 1
                    wv = bcT_sb[:].rearrange("p (kt m) -> p kt m", kt=KD)
                    for kt in range(KD):
                        mm = tensor.matmul(bcp_ps[:], wv[:, kt, :],
                                           xnv[:, kt, :],
                                           start=(kt == 0), stop=(kt == KD - 1))
                    mm.then_inc(spj.h, 1)
                spj.bump(2 * KI + 1)
                self.sec(tensor=f_proj)

                def f_xi(scalar, spj_base=spj_base, sl=sl, c=c):
                    scalar.wait_ge(stl.h, stl_base0 + c + 1)
                    for mt in range(KI):
                        scalar.wait_ge(spj.h, spj_base + mt + 1)
                        scalar.copy(xiv[:, mt, 3:3 + TC],
                                    proj_ps[:, mt % 2, :]).then_inc(sevx.h, 1)
                sevx.bump(KI)
                self.sec(scalar=f_xi)

                sevd_base = sevd.v
                dt_full = dt_st[0][:]
                xc_full = xc_st[0][:]
                ct_full = ctmp_b[:]

                def f_dtv(vector, spj_base=spj_base, sevd_base=sevd_base,
                          svd_base=svdt.v):
                    dtv = dt_st[0][:].rearrange("p (k t) -> p k t", k=KI)
                    for mt in range(KI):
                        g = KI + mt
                        vector.wait_ge(spj.h, spj_base + g + 1)
                        vector.tensor_scalar_add(
                            dtv[:, mt, :], proj_ps[:, g % 2, :],
                            self.ppc("dtb", mt)).then_inc(sevd.h, 1)
                    vector.wait_ge(sevd.h, sevd_base + KI)
                    vector.scalar_tensor_tensor(
                        xc_full, dt_full, -1.0, dt_full,
                        ALU.mult, ALU.max).then_inc(svdt.h, 1)
                    vector.wait_ge(svdt.h, svd_base + 1)
                    vector.tensor_scalar_max(dt_full, dt_full,
                                             0.0).then_inc(svdt.h, 1)
                sevd.bump(KI)

                def f_dts(scalar, svd_base=svdt.v):
                    scalar.wait_ge(svdt.h, svd_base + 1)
                    scalar.activation(xc_full, xc_full, AF.Exp,
                                      scale=self.ppc("c_m1"))
                    scalar.drain()
                    scalar.activation(xc_full, xc_full, AF.Ln, bias=1.0)
                    scalar.drain().then_inc(sact.h, 1)
                v_act = sact.bump()
                self.sec(vector=f_dtv, scalar=f_dts)

                def f_bc(scalar, spj_base=spj_base):
                    scalar.wait_ge(spj.h, spj_base + 2 * KI + 1)
                    scalar.copy(bc_st[0][:], bcp_ps[:]).then_inc(sbc.h, 1)
                v_bc = sbc.bump()
                self.sec(scalar=f_bc)

                cw3 = self.ppr("convw").rearrange("p (k j) -> p k j", k=KI)
                ct3 = ctmp_b[:].rearrange("p (k t) -> p k t", k=KI)
                tm3 = xsb[0][:].bitcast(BF16).rearrange("p (k t) -> p k t",
                                                        k=KI)

                def f_cv(vector, c=c, sevx_base=sevx_base, scv_base=scv.v,
                         sv=None):
                    sv = scv_base
                    vector.wait_ge(sevx.h, sevx_base + KI)
                    vector.wait_ge(stl.h, stl_base0 + c + 1)
                    vector.tensor_mul(
                        ct3, xiv[:, :, 0:TC],
                        cw3[:, :, 0:1].broadcast_to((128, KI, TC))
                    ).then_inc(scv.h, 1)
                    sv += 1
                    for j in range(1, 4):
                        vector.tensor_mul(
                            tm3, xiv[:, :, j:j + TC],
                            cw3[:, :, j:j + 1].broadcast_to((128, KI, TC))
                        ).then_inc(scv.h, 1)
                        sv += 1
                        vector.wait_ge(scv.h, sv)
                        vector.tensor_add(ct3, ct3, tm3).then_inc(scv.h, 1)
                        sv += 1
                        vector.wait_ge(scv.h, sv)
                    vector.tensor_add(
                        ct3, ct3,
                        self.ppr("convb").unsqueeze(2).broadcast_to(
                            (128, KI, TC))).then_inc(scv.h, 1)

                def f_fin(vector, v_act=v_act, svd_base=svdt.v - 2):
                    vector.wait_ge(sact.h, v_act)
                    vector.tensor_add(dt_full, dt_full,
                                      xc_full).then_inc(svdt.h, 1)
                svdt.bump(3)
                scv.bump(8)
                self.sec(vector=f_cv)
                self.sec(vector=f_fin)

                def f_cs(scalar, scv_t=scv.v, svd_t=svdt.v):
                    scalar.wait_ge(scv.h, scv_t)
                    scalar.wait_ge(svdt.h, svd_t)
                    scalar.activation(xc_full, ct_full,
                                      AF.Silu).then_inc(sxc.h, 1)
                sxc.bump(1)
                self.sec(scalar=f_cs)

                def f_st(sync, c=c, t0=t0, v_bc=v_bc,
                         sdt_t=svdt.v, sxc_t=sxc.v):
                    sync.wait_ge(svdt.h, sdt_t)
                    sync.wait_ge(sxc.h, sxc_t)
                    sync.wait_ge(sbc.h, v_bc)
                    sync.dma_start(out=sdtv[:, :, t0:t0 + TC],
                                   in_=dt_st[0][:].rearrange(
                                       "p (k t) -> p k t", k=KI)).then_inc(sst[0].h, 16)
                    sync.dma_start(out=sxcv[:, :, t0:t0 + TC],
                                   in_=xc_st[0][:].rearrange(
                                       "p (k t) -> p k t", k=KI)).then_inc(sst[0].h, 16)
                    sync.dma_start(out=self.s_bc[:, t0:t0 + TC],
                                   in_=bc_st[0][:]).then_inc(sst[0].h, 16)
                sst[0].bump(48)
                self.sec(sync=f_st)

            def f_wait(sync):
                sync.wait_ge(sst[0].h, sst[0].v)
            self.sec(sync=f_wait)

    def mamba2(self, src, dst):
        self.begin_phase()
        NCH = self.NCH
        with contextlib.ExitStack() as pst:
            sbp = lambda n, s, d=F32: self.sb("m2" + n, s, d, pst)
            dtb = sbp("dt", [128, KI * TC], BF16)
            xcb = sbp("xc", [128, KI * TC], BF16)
            bcbB = sbp("bcB", [NS, TC], BF16)
            bcbC = sbp("bcC", [NS, TC], BF16)
            dxb = sbp("dx", [128, KI * TC], BF16)
            Bb = sbp("Bb", [128, NS * TC], BF16)
            Cb = sbp("Cb", [128, NS * TC], BF16)
            hb = [sbp(f"hb{i}", [128, NS * TC], BF16) for i in range(2)]
            sel16_sb = sbp("sel", [NS, NS * 128], BF16)
            yred = [sbp(f"yr{i}", [128, TC]) for i in range(2)]
            a_big = sbp("ab", [128, NS * TC])
            ub2 = sbp("ub", [128, NS * TC], BF16)
            tmpc = sbp("tc", [128, 16])
            opT_sb = sbp("op", [128, KI * D], BF16)

            bb_ps = self.ps("m2bb", [128, 4, TC], pst)
            op_ps = self.ps("m2oq", [128, 2, TC], pst)

            ldc = self.gsem("hw")
            wl = self.gsem("sw")
            sbb = self.gsem("c")
            sbe = self.gsem("c")
            sa = self.gsem("c")
            ssc = self.gsem("c")
            spl = self.gsem("c")
            syr = self.gsem("c")
            sop = self.gsem("c")
            sx1 = self.gsem("c")
            sst = [self.gsem("hw")]
            schk = self.gsem("c")
            sdx = self.gsem("c")
            svp = self.gsem("c")
            sstc = self.gsem("c")
            sxr = self.gsem("hw")
            schk_base0 = schk.v
            sst_base0 = sst[0].v
            ssc_g0 = ssc.v
            sa_g0 = sa.v

            def f_w(gpsimd):
                gpsimd.wait_ge(self.ph.h, self.ph.v)
                gpsimd.dma_start(
                    out=opT_sb[:].rearrange("p (kt m) -> p kt m", kt=KI),
                    in_=self.opT.rearrange("(kt p) m -> p kt m", p=128),
                ).then_inc(wl.h, 16)
                gpsimd.dma_start(out=sel16_sb[:], in_=self.sel16).then_inc(wl.h, 16)
            wl.bump(32)
            self.sec(gpsimd=f_w)

            srcv = src.rearrange("(kt p) t -> p kt t", p=128)
            dstv = dst.rearrange("(kt p) t -> p kt t", p=128)
            sdtv = self.s_dt.rearrange("(kt p) t -> p kt t", p=128)
            sxcv = self.s_xc.rearrange("(kt p) t -> p kt t", p=128)

            ab3 = a_big[:].rearrange("p (s t) -> p s t", s=NS)
            ubv = ub2[:].rearrange("p (u s t) -> p u s t", u=2, s=8)

            for c in range(NCH):
                t0 = c * TC

                def f_ld(sync, c=c, t0=t0):
                    if c >= 1:
                        sync.wait_ge(schk.h, schk_base0 + c)
                    sync.dma_start(out=dtb[:].rearrange("p (k t) -> p k t", k=KI),
                                   in_=sdtv[:, :, t0:t0 + TC]).then_inc(ldc.h, 16)
                    sync.dma_start(out=xcb[:].rearrange("p (k t) -> p k t", k=KI),
                                   in_=sxcv[:, :, t0:t0 + TC]).then_inc(ldc.h, 16)
                    sync.dma_start(out=bcbB[:],
                                   in_=self.s_bc[0:NS, t0:t0 + TC]).then_inc(ldc.h, 16)
                    sync.dma_start(out=bcbC[:],
                                   in_=self.s_bc[NS:2 * NS, t0:t0 + TC]).then_inc(ldc.h, 16)
                v_ld = ldc.bump(64)
                self.sec(sync=f_ld)
                xres = dxb[:].bitcast(F32).rearrange("p (k t) -> p k t", k=KD)

                sbb_base = sbb.v
                sbe_base = sbe.v

                def f_bb(tensor, v_ld=v_ld, sbb_base=sbb_base, sbe_base=sbe_base, c=c):
                    tensor.wait_ge(ldc.h, v_ld)
                    if c == 0:
                        tensor.wait_ge(wl.h, wl.v)
                    for grp in range(8):
                        if c > 0 or grp >= 1:
                            tensor.wait_ge(sbe.h, sbe_base + grp)
                        mvg = bcbB if grp < 4 else bcbC
                        for j in range(4):
                            s = (grp % 4) * 4 + j
                            mm = tensor.matmul(
                                bb_ps[:, j, :],
                                sel16_sb[:, s * 128:(s + 1) * 128],
                                mvg[:], start=True, stop=True)
                        mm.then_inc(sbb.h, 1)
                sbb.bump(8)

                def f_be(scalar, sbb_base=sbb_base, c=c):
                    for grp in range(8):
                        scalar.wait_ge(sbb.h, sbb_base + grp + 1)
                        dst_b = (Bb if grp < 4 else Cb)
                        g4 = (grp % 4) * 4
                        scalar.copy(
                            dst_b[:].rearrange("p (s t) -> p s t", s=NS)[:, g4:g4 + 4, :],
                            bb_ps[:]).then_inc(sbe.h, 1)
                sbe.bump(8)
                self.sec(tensor=f_bb, scalar=f_be)

                def f_dx(vector, v_ld=v_ld):
                    vector.wait_ge(ldc.h, v_ld)
                    vector.tensor_mul(dxb[:], dtb[:], xcb[:]).then_inc(sdx.h, 1)
                v_dx = sdx.bump()
                self.sec(vector=f_dx)

                dtb3 = dtb[:].rearrange("p (k t) -> p k t", k=KI)
                dxb3 = dxb[:].rearrange("p (k t) -> p k t", k=KI)
                xcb3 = xcb[:].rearrange("p (k t) -> p k t", k=KI)
                ybf3 = xcb3

                sa_base = sa.v
                ssc_base = ssc.v
                spl_base = spl.v
                syr_base = syr.v
                sstc_base = sstc.v

                def f_a(scalar, v_ld=v_ld, c=c):
                    scalar.wait_ge(ldc.h, v_ld)
                    for st in range(2 * KI):
                        dhi, h = divmod(st, 2)
                        g = 32 * c + st
                        if g >= 2:
                            scalar.wait_ge(ssc.h, ssc_g0 + g - 1)
                        for j in range(8):
                            s = h * 8 + j
                            mm = scalar.activation(
                                ab3[:, s, :], dtb3[:, dhi, :], AF.Exp,
                                scale=self.ppc("A", dhi * NS + s))
                        mm.then_inc(sa.h, 1)
                sa.bump(2 * KI)

                def _stt(vector, dhi):
                    vector.scalar_tensor_tensor(
                        ybf3[:, dhi, :], xcb3[:, dhi, :],
                        self.ppc("Dm", dhi), yred[dhi % 2][:],
                        ALU.mult, ALU.add).then_inc(syr.h, 1)

                def f_scan(vector, c=c, sa_base=sa_base, ssc_base=ssc_base,
                           spl_base=spl_base, v_dx=v_dx, svp_base=svp.v,
                           sbe_base=sbe_base, sstc_base=sstc_base):
                    vector.wait_ge(sdx.h, v_dx)
                    vector.wait_ge(sbe.h, sbe_base + 8)
                    vector.wait_ge(sst[0].h, sst_base0 + 16 * c)
                    svp_v = svp_base
                    for dhi in range(KI):
                        par = dhi % 2
                        hb3 = hb[par][:].rearrange("p (s t) -> p s t", s=NS)
                        for h in range(2):
                            st = 2 * dhi + h
                            g = 32 * c + st
                            up = st % 2
                            s0 = h * 8
                            if g >= 2:
                                vector.wait_ge(ssc.h, ssc_g0 + g - 1)
                            vector.tensor_mul(
                                ubv[:, up, :, :],
                                dxb3[:, dhi:dhi + 1, :].broadcast_to(
                                    (128, 8, TC)),
                                Bb[:].rearrange("p (s t) -> p s t", s=NS)
                                [:, s0:s0 + 8, :]).then_inc(svp.h, 1)
                            svp_v += 1
                            vector.wait_ge(sa.h, sa_g0 + g + 1)
                            if c >= 1:
                                vector.wait_ge(sstc.h,
                                               sstc_base - KI + dhi + 1)
                            vector.tensor_mul(
                                tmpc[:, s0:s0 + 8].unsqueeze(2),
                                ab3[:, s0:s0 + 8, 0:1],
                                self.mst[:, dhi * NS + s0:dhi * NS + s0 + 8]
                                .unsqueeze(2)).then_inc(svp.h, 1)
                            svp_v += 1
                            vector.wait_ge(svp.h, svp_v)
                            vector.tensor_add(
                                ubv[:, up, :, 0:1],
                                ubv[:, up, :, 0:1],
                                tmpc[:, s0:s0 + 8].unsqueeze(2)
                            ).then_inc(svp.h, 1)
                            svp_v += 1
                            vector.memset(ab3[:, s0:s0 + 8, 0:1],
                                          0.0).then_inc(svp.h, 1)
                            svp_v += 1
                            vector.wait_ge(svp.h, svp_v)
                            if h == 0 and dhi >= 2:
                                vector.wait_ge(spl.h, spl_base + dhi - 1)
                            vector.tensor_tensor_scan(
                                hb[par][:, s0 * TC:(s0 + 8) * TC],
                                a_big[:, s0 * TC:(s0 + 8) * TC],
                                ub2[:, up * 8 * TC:(up + 1) * 8 * TC],
                                0.0, ALU.mult, ALU.add).then_inc(ssc.h, 1)
                        vector.wait_ge(ssc.h, ssc_g0 + 32 * c + 2 * dhi + 2)
                        vector.tensor_copy(self.mst[:, dhi * NS:(dhi + 1) * NS],
                                           hb3[:, :, TC - 1]).then_inc(sstc.h, 1)

                        def _tree(pd):
                            nonlocal svp_v
                            H = hb[pd % 2]
                            vector.wait_ge(spl.h, spl_base + pd + 1)
                            for w in (8, 4, 2):
                                vector.tensor_add(
                                    H[:, 0:w * TC], H[:, 0:w * TC],
                                    H[:, w * TC:2 * w * TC]).then_inc(svp.h, 1)
                                svp_v += 1
                                vector.wait_ge(svp.h, svp_v)
                            vector.tensor_add(yred[pd % 2][:], H[:, 0:TC],
                                              H[:, TC:2 * TC]).then_inc(svp.h, 1)
                            svp_v += 1

                        if dhi >= 1:
                            _tree(dhi - 1)
                        if dhi >= 2:
                            vector.wait_ge(svp.h, svp_v)
                            _stt(vector, dhi - 2)
                    _tree(KI - 1)
                    vector.wait_ge(svp.h, svp_v)
                    _stt(vector, KI - 2)
                    _stt(vector, KI - 1)
                ssc.bump(2 * KI)
                svp.bump(12 * KI)
                syr.bump(KI)
                sstc.bump(KI)

                def f_pl(gpsimd, ssc_base=ssc_base, sstc_base=sstc_base,
                         sbe_base=sbe_base, syr_base=syr_base):
                    gpsimd.wait_ge(sbe.h, sbe_base + 8)
                    for dhi in range(KI):
                        gpsimd.wait_ge(ssc.h, ssc_base + (dhi + 1) * 2)
                        gpsimd.wait_ge(sstc.h, sstc_base + dhi + 1)
                        if dhi >= 2:
                            gpsimd.wait_ge(syr.h, syr_base + dhi - 1)
                        gpsimd.tensor_mul(hb[dhi % 2][:], hb[dhi % 2][:],
                                          Cb[:]).then_inc(spl.h, 1)
                spl.bump(KI)
                self.sec(scalar=f_a, vector=f_scan, gpsimd=f_pl)

                def f_ldx(sync, c=c, t0=t0, ssc_t=ssc.v):
                    sync.wait_ge(ssc.h, ssc_t)
                    sync.dma_start(out=xres, in_=srcv[:, :, t0:t0 + TC]
                                   ).then_inc(sxr.h, 16)
                sxr.bump(16)
                self.sec(sync=f_ldx)

                sop_base = sop.v
                sx1_base = sx1.v

                def f_op(tensor, syr_t=syr.v, sop_base=sop_base,
                         sx1_base=sx1_base, c=c):
                    tensor.wait_ge(syr.h, syr_t)
                    for mt in range(KD):
                        bank = mt % 2
                        if mt >= 2:
                            tensor.wait_ge(sx1.h, sx1_base + mt - 1)
                        for kt in range(KI):
                            mm = tensor.matmul(
                                op_ps[:, bank, :],
                                opT_sb[:].rearrange("p (k m) -> p k m", k=KI)
                                [:, kt, mt * 128:(mt + 1) * 128],
                                ybf3[:, kt, :],
                                start=(kt == 0), stop=(kt == KI - 1))
                        mm.then_inc(sop.h, 1)
                sop.bump(KD)

                def f_x1(vector, sop_base=sop_base, c=c, v_xr=sxr.v,
                         sxr=sxr, xres=xres):
                    x1v = ub2[:].bitcast(F32).rearrange("p (k t) -> p k t", k=KD)
                    if c >= 1:
                        vector.wait_ge(sst[0].h, sst_base0 + 16 * c)
                    vector.wait_ge(sxr.h, v_xr)
                    for mt in range(KD):
                        vector.wait_ge(sop.h, sop_base + mt + 1)
                        vector.tensor_add(x1v[:, mt, :], xres[:, mt, :],
                                          op_ps[:, mt % 2, :]).then_inc(sx1.h, 1)
                    vector.drain().then_inc(schk.h, 1)
                sx1.bump(KD)
                schk.bump()
                self.sec(tensor=f_op, vector=f_x1)

                def f_st(sync, t0=t0, sx1_t=sx1.v):
                    sync.wait_ge(sx1.h, sx1_t)
                    sync.dma_start(out=dstv[:, :, t0:t0 + TC],
                                   in_=ub2[:].bitcast(F32).rearrange(
                                       "p (k t) -> p k t", k=KD)).then_inc(sst[0].h, 16)
                sst[0].bump(16)
                self.sec(sync=f_st)

            def f_wait(sync):
                sync.wait_ge(sst[0].h, sst[0].v)
            self.sec(sync=f_wait)

    def ffn_pre(self, src, call):
        self.begin_phase()
        NCH = self.NCH
        tg = f"f{call}p"
        with contextlib.ExitStack() as pst:
            sbp = lambda n, s, d=F32: self.sb(tg + n, s, d, pst)
            xsb = sbp("x", [128, KD * TC])
            lnb = self.mk_ln_bufs(pst, tg + "l", TC)
            u0 = sbp("u0", [128, KD * TC])
            n_f = sbp("nf", [128, KD * TC])
            n_b = sbp("nb", [128, KD * TC], BF16)
            x2 = sbp("x2", [128, KD * TC])
            n2b = sbp("n2", [128, KD * TC], BF16)
            yin = sbp("yi", [128, KD * TC])
            mixw_sb = sbp("mw", [128, KD * D], BF16)
            rw_sb = sbp("rw", [128, KD * NE], BF16)
            eye_sb = sbp("ey", [128, 128])
            b2s = sbp("b2", [NE, D], BF16)
            lg = sbp("lg", [8, TC])
            prb = sbp("pr", [128, NE])
            wo_ = sbp("wo", [128, NE])
            r1 = sbp("r1", [128, 1])
            r2 = sbp("r2", [128, 1])
            cmb = sbp("cm", [128, NE])
            cbT = sbp("cb", [NE, TC], BF16)
            mt_t = sbp("mt", [128, TC])

            mix_ps = self.ps(tg + "mp", [128, 2, TC], pst)
            rt_ps = self.ps(tg + "rp", [8, TC], pst)
            tt_ps = self.ps(tg + "tt", [128, NE + 128], pst)

            wl = self.gsem("sw")
            xl = self.gsem("hw")
            smix = self.gsem("c")
            sme = self.gsem("c")
            smb = self.gsem("c")
            srt = self.gsem("c")
            str_ = self.gsem("c")
            ssm = self.gsem("c")
            sse = self.gsem("c")
            scb = self.gsem("c")
            sb2 = self.gsem("c")
            syi = self.gsem("c")
            sx2 = self.gsem("c")
            sst = self.gsem("hw")

            def f_w(gpsimd):
                gpsimd.wait_ge(self.ph.h, self.ph.v)
                gpsimd.dma_start(
                    out=mixw_sb[:].rearrange("p (kt m) -> p kt m", kt=KD),
                    in_=self.mixw.rearrange("(kt p) m -> p kt m", p=128),
                ).then_inc(wl.h, 16)
                gpsimd.dma_start(
                    out=rw_sb[:].rearrange("p (kt m) -> p kt m", kt=KD),
                    in_=self.routerw.rearrange("(kt p) m -> p kt m", p=128),
                ).then_inc(wl.h, 16)
                gpsimd.dma_start(out=b2s[:], in_=self.b2).then_inc(wl.h, 16)
            wl.bump(48)

            wlh = self.gsem("hw")

            def f_w2(sync):
                sync.wait_ge(self.ph.h, self.ph.v)
                sync.dma_start(out=eye_sb[:], in_=self.eye32).then_inc(wlh.h, 16)
            wlh.bump(16)
            self.sec(sync=f_w2, gpsimd=f_w)

            srcv = src.rearrange("(kt p) t -> p kt t", p=128)
            xwv = self.x_work.rearrange("(kt p) t -> p kt t", p=128)
            n2v = self.s_n2.rearrange("(kt p) t -> p kt t", p=128)
            yav = self.s_ya.rearrange("(kt p) t -> p kt t", p=128)

            u0v = u0[:].rearrange("p (k t) -> p k t", k=KD)
            nfv = n_f[:].rearrange("p (k t) -> p k t", k=KD)
            nbv = n_b[:].rearrange("p (k t) -> p k t", k=KD)
            x2v = x2[:].rearrange("p (k t) -> p k t", k=KD)
            n2bv = n2b[:].rearrange("p (k t) -> p k t", k=KD)
            yiv = yin[:].rearrange("p (k t) -> p k t", k=KD)
            x3 = xsb[:].rearrange("p (k t) -> p k t", k=KD)

            done_prev = [0]
            for c in range(NCH):
                t0 = c * TC

                def f_ld(sync, c=c, t0=t0, dp=done_prev[0]):
                    if c >= 1:
                        sync.wait_ge(sst.h, dp)
                    sync.dma_start(out=x3, in_=srcv[:, :, t0:t0 + TC]
                                   ).then_inc(xl.h, 16)
                v_x = xl.bump(16)
                self.sec(sync=f_ld)

                def f_gate(eng, v=v_x, c=c, smix_t=smix.v, sst_t=sst.v):
                    eng.wait_ge(xl.h, v)
                    eng.wait_ge(wl.h, wl.v)
                    eng.wait_ge(wlh.h, wlh.v)
                    if c >= 1:
                        eng.wait_ge(smix.h, smix_t)
                        eng.wait_ge(sst.h, sst_t)
                self.sec(tensor=f_gate, vector=f_gate, scalar=f_gate)

                v_u0 = self.emit_ln(lnb, x3, TC, out_f32=u0v)
                v_n = self.emit_ln(lnb, u0v, TC, out_f32=nfv, out_b16=nbv,
                                   g=self.ppr("g1"), b=self.ppr("be1"),
                                   wait_sv=v_u0)

                smix_base = smix.v
                sme_base = sme.v

                def f_mm(tensor, v_n=v_n, smix_base=smix_base,
                         sme_base=sme_base, c=c, syi_t=syi.v):
                    tensor.wait_ge(lnb["sv"].h, v_n)
                    if c >= 1:
                        tensor.wait_ge(syi.h, syi_t)
                    wv = mixw_sb[:].rearrange("p (kt m) -> p kt m", kt=KD)
                    for mt in range(KD):
                        if mt >= 2:
                            tensor.wait_ge(sme.h, sme_base + mt - 1)
                        for kt in range(KD):
                            mm = tensor.matmul(
                                mix_ps[:, mt % 2, :],
                                wv[:, kt, mt * 128:(mt + 1) * 128],
                                nbv[:, kt, :], start=(kt == 0),
                                stop=(kt == KD - 1))
                        mm.then_inc(smix.h, 1)
                smix.bump(KD)

                def f_me(vector, smix_base=smix_base, sme_base=sme.v,
                         smb_base=smb.v, c=c):
                    for mt in range(KD):
                        vector.wait_ge(smix.h, smix_base + mt + 1)
                        vector.tensor_scalar(x2v[:, mt, :], mix_ps[:, mt % 2, :],
                                             self.ppc("mixb", mt), None,
                                             ALU.add).then_inc(sme.h, 1)
                    vector.wait_ge(sme.h, sme_base + KD)
                    vector.tensor_add(x2[:], x2[:], u0[:]).then_inc(smb.h, 1)
                    vector.wait_ge(smb.h, smb_base + 1)
                    vector.tensor_add(x2[:], x2[:], n_f[:]).then_inc(smb.h, 1)
                sme.bump(KD)
                smb.bump(2)
                self.sec(tensor=f_mm, vector=f_me)

                def f_g2gate(eng, sme_t=sme.v, smb_t=smb.v):
                    eng.wait_ge(sme.h, sme_t)
                    eng.wait_ge(smb.h, smb_t)
                self.sec(vector=f_g2gate, scalar=f_g2gate, tensor=f_g2gate)
                v_n2 = self.emit_ln(lnb, x2v, TC, out_b16=n2bv,
                                    g=self.ppr("g2"), b=self.ppr("be2"))

                def f_rt(tensor, v_n2=v_n2):
                    tensor.wait_ge(lnb["sv"].h, v_n2)
                    rv = rw_sb[:].rearrange("p (kt m) -> p kt m", kt=KD)
                    for kt in range(KD):
                        mm = tensor.matmul(rt_ps[:], rv[:, kt, :],
                                           n2bv[:, kt, :],
                                           start=(kt == 0), stop=(kt == KD - 1))
                    mm.then_inc(srt.h, 1)
                v_rt = srt.bump()
                self.sec(tensor=f_rt)

                str_base = str_.v
                ssm_base = ssm.v
                sse_base = sse.v
                scb_base = scb.v

                def f_rte(scalar, v_rt=v_rt):
                    scalar.wait_ge(srt.h, v_rt)
                    scalar.copy(lg[:], rt_ps[:])
                    scalar.drain().then_inc(sse.h, 1)
                v_lg = sse.bump()

                def f_tr(tensor, v_lg=v_lg, str_base=str_base,
                         scb_base=scb_base, ssm_base=ssm_base):
                    tensor.wait_ge(sse.h, v_lg)
                    for q in range(TC // 128):
                        if q >= 1:
                            tensor.wait_ge(scb.h, scb_base + q)
                        mm = tensor.transpose(
                            tt_ps[:, 0:NE], lg[:, q * 128:(q + 1) * 128],
                            eye_sb[0:NE, 0:NE])
                        mm.then_inc(str_.h, 1)
                        tensor.wait_ge(ssm.h, ssm_base + 2 * q + 2)
                        mm = tensor.transpose(tt_ps[0:NE, NE:NE + 128], cmb[:],
                                              eye_sb[:])
                        mm.then_inc(str_.h, 1)
                str_.bump(2 * (TC // 128))

                def f_sm(vector, str_base=str_base, sse_base=sse_base,
                         scb_base=scb_base, ssm_base=ssm_base):
                    for q in range(TC // 128):
                        vector.wait_ge(str_.h, str_base + 2 * q + 1)
                        vector.tensor_copy(prb[:], tt_ps[:, 0:NE])
                        vector.drain()
                        vector.reduce_max(r1[:], prb[:], axis=AX.X, negate=True)
                        vector.drain().then_inc(ssm.h, 1)
                        vector.wait_ge(sse.h, sse_base + 1 + q + 1)
                        vector.reduce_sum(r2[:], prb[:], axis=AX.X)
                        vector.drain()
                        vector.reciprocal(r2[:], r2[:])
                        vector.drain()
                        vector.tensor_scalar(prb[:], prb[:], r2[:], None,
                                             ALU.mult)
                        vector.drain()
                        vector.reduce_max(r1[:], prb[:], axis=AX.X)
                        vector.drain()
                        vector.tensor_scalar(wo_[:], prb[:], r1[:], None,
                                             ALU.is_ge)
                        vector.drain()
                        vector.scalar_tensor_tensor(wo_[:], wo_[:], -1e30,
                                                    prb[:], ALU.mult, ALU.add)
                        vector.drain()
                        vector.reduce_max(r2[:], wo_[:], axis=AX.X)
                        vector.drain()
                        vector.tensor_scalar(wo_[:], prb[:], r2[:], None,
                                             ALU.is_ge)
                        vector.drain()
                        if q >= 1:
                            vector.wait_ge(str_.h, str_base + 2 * q)
                        vector.tensor_mul(cmb[:], prb[:], wo_[:])
                        vector.drain().then_inc(ssm.h, 1)
                ssm.bump(2 * (TC // 128))

                def f_se(scalar, ssm_base=ssm_base, sse_base=sse_base,
                         str_base=str_base):
                    for q in range(TC // 128):
                        scalar.wait_ge(ssm.h, ssm_base + 2 * q + 1)
                        scalar.activation(prb[:], prb[:], AF.Exp, bias=r1[:])
                        scalar.drain().then_inc(sse.h, 1)
                        scalar.wait_ge(str_.h, str_base + 2 * q + 2)
                        scalar.copy(cbT[:, q * 128:(q + 1) * 128],
                                    tt_ps[:NE, NE:NE + 128])
                        scalar.drain().then_inc(scb.h, 1)
                sse.bump(TC // 128)
                scb.bump(TC // 128)
                self.sec(scalar=f_rte)
                self.sec(tensor=f_tr, vector=f_sm, scalar=f_se)

                sb2_base = sb2.v
                syi_base = syi.v

                def f_b2(tensor, scb_t=scb.v, sb2_base=sb2_base,
                         syi_base=syi_base, sme_t=sme.v):
                    tensor.wait_ge(scb.h, scb_t)
                    tensor.wait_ge(sme.h, sme_t)
                    for mt in range(KD):
                        if mt >= 2:
                            tensor.wait_ge(syi.h, syi_base + mt - 1)
                        mm = tensor.matmul(mix_ps[:, mt % 2, :],
                                           b2s[:, mt * 128:(mt + 1) * 128],
                                           cbT[:], start=True, stop=True)
                        mm.then_inc(sb2.h, 1)
                sb2.bump(KD)

                def f_yi(vector, sb2_base=sb2_base, syi_base=syi.v,
                         smb_base=smb.v):
                    for mt in range(KD):
                        vector.wait_ge(sb2.h, sb2_base + mt + 1)
                        vector.tensor_add(yiv[:, mt, :], x3[:, mt, :],
                                          mix_ps[:, mt % 2, :]).then_inc(syi.h, 1)
                    vector.wait_ge(syi.h, syi_base + KD)
                    vector.tensor_add(yin[:], yin[:], x2[:]).then_inc(smb.h, 1)
                syi.bump(KD)
                smb.bump(1)
                self.sec(tensor=f_b2, vector=f_yi)

                def f_st(sync, c=c, t0=t0, v_n2=v_n2, syi_t=syi.v,
                         scb_t=scb.v, smb_t=smb.v):
                    sync.wait_ge(lnb["sv"].h, v_n2)
                    sync.wait_ge(syi.h, syi_t)
                    sync.wait_ge(smb.h, smb_t)
                    sync.wait_ge(scb.h, scb_t)
                    sync.dma_start(out=xwv[:, :, t0:t0 + TC], in_=x2v
                                   ).then_inc(sst.h, 16)
                    sync.dma_start(out=n2v[:, :, t0:t0 + TC], in_=n2bv
                                   ).then_inc(sst.h, 16)
                    sync.dma_start(out=yav[:, :, t0:t0 + TC], in_=yiv
                                   ).then_inc(sst.h, 16)
                    sync.dma_start(out=self.s_cb[:, t0:t0 + TC], in_=cbT[:]
                                   ).then_inc(sst.h, 16)
                done_prev[0] = sst.bump(64)
                self.sec(sync=f_st)

            def f_wait(sync):
                sync.wait_ge(sst.h, sst.v)
            self.sec(sync=f_wait)

    def ffn_exp(self, dst, call):
        self.begin_phase()
        NCH = self.NCH
        tg = f"f{call}e"
        with contextlib.ExitStack() as pst:
            sbp = lambda n, s, d=F32: self.sb(tg + n, s, d, pst)
            w1sb = sbp("w1", [128, KD * FF], BF16)
            w2sb = sbp("w2", [128, KF * D], BF16)
            n2s = [sbp(f"n2{i}", [128, KD * TC], BF16) for i in range(2)]
            h_sb = sbp("h", [128, KF * TC], BF16)
            cmbs = [sbp(f"cT{i}", [NE, TC], BF16) for i in range(2)]
            sel8s = sbp("s8", [NE, NE * 128], BF16)
            cbb = [sbp(f"cb{i}", [128, TC]) for i in range(2)]
            yab = sbp("ya", [128, KD * TC])
            tmp = sbp("tp", [128, TC])

            h_ps = self.ps(tg + "hp", [128, 2, TC], pst)
            y_ps = self.ps(tg + "yp", [128, 4, TC], pst)
            cb_ps = self.ps(tg + "cp", [128, TC], pst)

            wl1p = [self.gsem("sw") for _ in range(4)]
            wl2p = [self.gsem("sw") for _ in range(2)]
            cl = self.gsem("sw")
            nl = [self.gsem("hw") for i in range(2)]
            yl = self.gsem("hw")
            shp = self.gsem("c")
            she = self.gsem("c")
            syp = self.gsem("c")
            sfc = self.gsem("c")
            sye = self.gsem("c")
            smu = self.gsem("c")
            scb = self.gsem("c")
            sce = self.gsem("c")
            sstY = self.gsem("hw")
            sA = self.gsem("c")
            wl1_b = [s.v for s in wl1p]
            wl2_b = [s.v for s in wl2p]
            shp_b = shp.v
            she_b = she.v
            syp_b = syp.v
            sye_b = sye.v
            smu_b = smu.v
            sce_b = sce.v

            n2v = self.s_n2.rearrange("(kt p) t -> p kt t", p=128)
            yav = self.s_ya.rearrange("(kt p) t -> p kt t", p=128)
            dstv = dst.rearrange("(kt p) t -> p kt t", p=128)
            w1v = w1sb[:].rearrange("p (kt m) -> p kt m", kt=KD)
            w2v = w2sb[:].rearrange("p (kt m) -> p kt m", kt=KF)
            h3 = h_sb[:].rearrange("p (kt t) -> p kt t", kt=KF)
            yabv = yab[:].rearrange("p (k t) -> p k t", k=KD)

            def f_cl(gpsimd):
                gpsimd.wait_ge(self.ph.h, self.ph.v)
                gpsimd.dma_start(out=sel8s[:], in_=self.sel8).then_inc(cl.h, 16)
            cl.bump(16)
            self.sec(gpsimd=f_cl)

            def f_ws(gpsimd):
                gpsimd.wait_ge(self.ph.h, self.ph.v)
                for e in range(NE):
                    for part in range(4):
                        if e >= 1:
                            gpsimd.wait_ge(
                                shp.h, shp_b + KF * (e * NCH - 1)
                                + 8 * (part + 1))
                        gpsimd.dma_start(
                            out=w1v[:, :, part * (FF // 4):(part + 1) * (FF // 4)],
                            in_=self.w1.rearrange("e (kt p) m -> e p kt m", p=128)
                            [e, :, :, part * (FF // 4):(part + 1) * (FF // 4)],
                        ).then_inc(wl1p[part].h, 16)
                    for part in range(2):
                        if e >= 1:
                            gpsimd.wait_ge(
                                syp.h, syp_b + 8 * (e * NCH - 1)
                                + 4 * (part + 1))
                        gpsimd.dma_start(
                            out=w2sb[:].rearrange("p (kt m) -> p kt m", kt=KF)
                            [:, :, part * (D // 2):(part + 1) * (D // 2)],
                            in_=self.w2.rearrange("e (kt p) m -> e p kt m", p=128)
                            [e, :, :, part * (D // 2):(part + 1) * (D // 2)],
                        ).then_inc(wl2p[part].h, 16)
            for s_ in wl1p:
                s_.bump(16 * NE)
            for s_ in wl2p:
                s_.bump(16 * NE)
            self.sec(gpsimd=f_ws)

            for e in range(NE):
                for sc in range(NCH):
                    t0 = sc * TC
                    sl = (e * NCH + sc) % 2
                    cb_sl = (e * NCH + sc) % 2
                    esc = e * NCH + sc

                    def f_n2(scalar, e=e, sc=sc, t0=t0, sl=sl, esc=esc,
                             scb_t=scb.v):
                        if esc >= 2:
                            scalar.wait_ge(shp.h, shp_b + KF * (esc - 1))
                            scalar.wait_ge(scb.h, scb_t - 1)
                        scalar.dma_start(out=n2s[sl][:].rearrange(
                            "p (k t) -> p k t", k=KD),
                            in_=n2v[:, :, t0:t0 + TC]).then_inc(nl[sl].h, 16)
                        scalar.dma_start(out=cmbs[sl][:],
                                       in_=self.s_cb[:, t0:t0 + TC]
                                       ).then_inc(nl[sl].h, 16)
                    v_n2 = nl[sl].bump(32)

                    def f_ya(sync, t0=t0, esc=esc, sstY_t=sstY.v):
                        if esc >= 1:
                            sync.wait_ge(sstY.h, sstY_t)
                        sync.dma_start(out=yabv, in_=yav[:, :, t0:t0 + TC]
                                       ).then_inc(yl.h, 16)
                    v_ya = yl.bump(16)
                    self.sec(scalar=f_n2)

                    def f_cb(tensor, e=e, t0=t0, sl=sl, esc=esc, v_cl=cl.v,
                             sce_t=sce.v, v_n2=v_n2):
                        tensor.wait_ge(cl.h, v_cl)
                        if esc >= 1:
                            tensor.wait_ge(sce.h, sce_b + esc)
                        tensor.wait_ge(nl[sl].h, v_n2)
                        mm = tensor.matmul(
                            cb_ps[:], sel8s[:, e * 128:(e + 1) * 128],
                            cmbs[sl][:], start=True, stop=True)
                        mm.then_inc(scb.h, 1)
                    v_cb = scb.bump()

                    def f_cbe(vector, sl=cb_sl, v_cb=v_cb, esc=esc):
                        vector.wait_ge(scb.h, v_cb)
                        if esc >= 2:
                            vector.wait_ge(smu.h, smu_b + 8 * (esc - 1))
                        vector.tensor_copy(cbb[sl][:], cb_ps[:]).then_inc(sce.h, 1)
                    sce.bump()
                    self.sec(tensor=f_cb, vector=f_cbe)

                    shp_base = shp.v
                    she_base = she.v

                    def f_A(tensor, e=e, sl=sl, esc=esc, v_n2=v_n2,
                            shp_base=shp_base, she_base=she_base):
                        tensor.wait_ge(nl[sl].h, v_n2)
                        n2t = n2s[sl][:].rearrange("p (k t) -> p k t", k=KD)
                        if esc >= 1:
                            tensor.wait_ge(she.h, she_b + KF * esc)
                        for ft in range(KF):
                            if ft % 8 == 0:
                                p_ = ft // 8
                                tensor.wait_ge(wl1p[p_].h,
                                               wl1_b[p_] + 16 * (e + 1))
                            if ft >= 2:
                                tensor.wait_ge(she.h, she_base + ft - 1)
                            for kt in range(KD):
                                mm = tensor.matmul(
                                    h_ps[:, ft % 2, :],
                                    w1v[:, kt, ft * 128:(ft + 1) * 128],
                                    n2t[:, kt, :],
                                    start=(kt == 0), stop=(kt == KD - 1))
                            mm.then_inc(shp.h, 1)
                    shp.bump(KF)

                    def f_he(scalar, e=e, shp_base=shp_base, esc=esc):
                        if esc >= 1:
                            scalar.wait_ge(syp.h, syp_b + 8 * esc)
                        for ft in range(KF):
                            scalar.wait_ge(shp.h, shp_base + ft + 1)
                            scalar.activation(h3[:, ft, :], h_ps[:, ft % 2, :],
                                              AF.Silu,
                                              bias=self.ppc("b1", e * KF + ft)
                                              ).then_inc(she.h, 1)
                        scalar.drain().then_inc(sA.h, 1)
                    she.bump(KF)
                    sA.bump()
                    self.sec(tensor=f_A, scalar=f_he)
                    self.sec(sync=f_ya)

                    syp_base = syp.v
                    sye_base = sye.v
                    smu_base = smu.v

                    def f_B(tensor, e=e, esc=esc, she_b2=she.v - KF,
                            syp_base=syp_base, sye_base=sye_base):
                        if esc >= 1:
                            tensor.wait_ge(smu.h, smu_b + 8 * esc)
                        for half in range(2):
                            tensor.wait_ge(wl2p[half].h,
                                           wl2_b[half] + 16 * (e + 1))
                            for mi in range(4):
                                mt = half * 4 + mi
                                gy = half * 4 + mi
                                if gy >= 4:
                                    tensor.wait_ge(smu.h, smu_base + gy - 4 + 1)
                                for kt in range(KF):
                                    if gy == 0:
                                        tensor.wait_ge(she.h, she_b2 + kt + 1)
                                    mm = tensor.matmul(
                                        y_ps[:, mi, :],
                                        w2v[:, kt, mt * 128:(mt + 1) * 128],
                                        h3[:, kt, :],
                                        start=(kt == 0), stop=(kt == KF - 1))
                                mm.then_inc(syp.h, 1)
                    syp.bump(8)

                    def f_ye(vector, sl=cb_sl, esc=esc, v_ya=v_ya,
                             syp_base=syp_base, sce_t=sce.v):
                        vector.wait_ge(yl.h, v_ya)
                        vector.wait_ge(sce.h, sce_t)
                        for gy in range(8):
                            mi = gy % 4
                            mt = gy
                            vector.wait_ge(syp.h, syp_base + gy + 1)
                            if gy >= 1:
                                vector.drain()
                            vector.tensor_mul(tmp[:], y_ps[:, mi, :],
                                              cbb[sl][:]).then_inc(smu.h, 1)
                            vector.drain()
                            vector.tensor_add(yabv[:, mt, :], yabv[:, mt, :],
                                              tmp[:]).then_inc(sye.h, 1)
                    sye.bump(8)
                    smu.bump(8)
                    self.sec(tensor=f_B, vector=f_ye)

                    def f_st(sync, e=e, t0=t0, sye_t=sye.v):
                        sync.wait_ge(sye.h, sye_t)
                        out = dstv if e == NE - 1 else yav
                        sync.dma_start(out=out[:, :, t0:t0 + TC], in_=yabv
                                       ).then_inc(sstY.h, 16)
                    sstY.bump(16)
                    self.sec(sync=f_st)

            def f_wait(sync):
                sync.wait_ge(sstY.h, sstY.v)
            self.sec(sync=f_wait)

    def rwkv1(self, src):
        self.begin_phase()
        NCH = self.NCH
        TW = TC + 2
        tg = "r1"
        with contextlib.ExitStack() as pst:
            sbp = lambda n, s, d=F32: self.sb(tg + n, s, d, pst)
            xw = sbp("xw", [128, KD * TW])
            car = sbp("car", [128, KD])
            lnb = self.mk_ln_bufs(pst, tg + "l", TC)
            xm_b = sbp("xm", [128, KD * TC], BF16)
            wk_sb = sbp("wk", [128, KD * D], BF16)
            wv_sb = sbp("wv", [128, KD * D], BF16)
            wr_sb = sbp("wr", [128, KD * D], BF16)
            wo_sb = sbp("wo", [128, KD * D], BF16)
            k_sb = sbp("k", [128, KD * TC])
            v_sb = sbp("v", [128, KD * TC])
            sr_sb = sbp("sr", [128, KD * TC])
            zb = sbp("z", [128, KD * TC], BF16)
            kv_t = sbp("kv", [128, TC])
            ab_buf = sbp("ab", [128, TW - 1])
            bb_buf = sbp("bb", [128, TW - 1])
            t1 = sbp("t1", [128, TC])
            t2 = sbp("t2", [128, TC])
            tb1 = sbp("tb", [128, KD * TC], BF16)

            kvr_ps = self.ps(tg + "kp", [128, 2, TC], pst)
            ao_ps = self.ps(tg + "ap", [128, 2, TC], pst)

            wl = self.gsem("sw")
            xl = self.gsem("hw")
            spj = self.gsem("c")
            sev = self.gsem("c")
            swk = self.gsem("c")
            sao = self.gsem("c")
            su2 = self.gsem("c")
            sst = self.gsem("hw")
            smx = self.gsem("c")
            smv = self.gsem("c")
            sza = self.gsem("c")

            def f_w(gpsimd):
                gpsimd.wait_ge(self.ph.h, self.ph.v)
                for dst_sb, srcw in ((wk_sb, self.WkT), (wv_sb, self.WvT),
                                     (wr_sb, self.WrT), (wo_sb, self.WoT)):
                    gpsimd.dma_start(
                        out=dst_sb[:].rearrange("p (kt m) -> p kt m", kt=KD),
                        in_=srcw.rearrange("(kt p) m -> p kt m", p=128),
                    ).then_inc(wl.h, 16)
            wl.bump(64)
            self.sec(gpsimd=f_w)

            srcv = src.rearrange("(kt p) t -> p kt t", p=128)
            u2v = self.s_u2.rearrange("(kt p) t -> p kt t", p=128)
            xwv = xw[:].rearrange("p (k t) -> p k t", k=KD)
            u1v = xwv
            xmv = xm_b[:].rearrange("p (k t) -> p k t", k=KD)
            kv3 = k_sb[:].rearrange("p (k t) -> p k t", k=KD)
            vv3 = v_sb[:].rearrange("p (k t) -> p k t", k=KD)
            sr3 = sr_sb[:].rearrange("p (k t) -> p k t", k=KD)
            zb3 = zb[:].rearrange("p (k t) -> p k t", k=KD)

            ln_done = {}
            for c in range(NCH):
                t0 = c * TC

                def f_ld(sync, c=c, t0=t0, sst_t=sst.v):
                    if c >= 1:
                        sync.wait_ge(sst.h, sst_t)
                    lo = max(t0 - 1, 0)
                    hi = min(t0 + TC + 1, self.T)
                    off = 1 if c == 0 else 0
                    sync.dma_start(
                        out=xwv[:, :, off:off + (hi - lo)],
                        in_=srcv[:, :, lo:hi]).then_inc(xl.h, 16)
                v_x = xl.bump(16)
                self.sec(sync=f_ld)

                def f_gate(eng, v=v_x, c=c, smx_t=smx.v):
                    eng.wait_ge(xl.h, v)
                    eng.wait_ge(wl.h, wl.v)
                    if c >= 1:
                        eng.wait_ge(smx.h, smx_t)
                self.sec(tensor=f_gate, vector=f_gate, scalar=f_gate)

                def f_car(vector, c=c):
                    if c == 0:
                        vector.memset(u1v[:, :, 0:1], 0.0)
                    else:
                        vector.tensor_copy(u1v[:, :, 0:1],
                                           car[:].unsqueeze(2))
                    vector.drain()
                self.sec(vector=f_car)

                ctr = lambda a: a[:, :, 1:1 + TC]
                v_u0 = self.emit_ln(lnb, ctr(xwv), TC, out_f32=ctr(u1v))
                v_u1 = self.emit_ln(lnb, ctr(u1v), TC, out_f32=ctr(u1v),
                                    g=self.ppr("ln1g"), b=self.ppr("ln1b"),
                                    wait_sv=v_u0)
                if c < NCH - 1:
                    hb = self._halo_bufs(lnb)
                    v_h0 = self.emit_ln(hb, xwv[:, :, TW - 1:TW], 1,
                                        out_f32=u1v[:, :, TW - 1:TW],
                                        wait_sv=v_u1)
                    v_u1 = self.emit_ln(hb, u1v[:, :, TW - 1:TW], 1,
                                        out_f32=u1v[:, :, TW - 1:TW],
                                        g=self.ppr("ln1g"), b=self.ppr("ln1b"),
                                        wait_sv=v_h0)
                else:
                    def f_hz(vector):
                        vector.memset(u1v[:, :, TW - 1:TW], 0.0)
                        vector.drain()
                    self.sec(vector=f_hz)

                def f_sc(vector, v_u1=v_u1):
                    vector.tensor_copy(car[:].unsqueeze(2), u1v[:, :, TC:TC + 1])
                    vector.drain()
                self.sec(vector=f_sc)

                def f_mx(vector, smv_base=smv.v):
                    sv = smv_base
                    cur = u1v[:, :, 1:1 + TC]
                    prv = u1v[:, :, 0:TC]
                    nxt = u1v[:, :, 2:2 + TC]
                    HB = KD // 2
                    bc = lambda n: self.ppr(n).unsqueeze(2).broadcast_to(
                        (128, KD, TC))
                    tb3 = tb1[:].rearrange("p (k t) -> p k t", k=KD)
                    vector.tensor_mul(tb3, cur, bc("tmix")).then_inc(smv.h, 1)
                    vector.tensor_mul(xmv, prv, bc("tmix1m")).then_inc(smv.h, 1)
                    sv += 2
                    vector.wait_ge(smv.h, sv)
                    vector.tensor_add(tb3, tb3, xmv).then_inc(smv.h, 1)
                    sv += 1
                    vector.wait_ge(smv.h, sv)
                    cmb_ = bc("cmix")
                    vector.tensor_mul(xmv[:, 0:HB, :], prv[:, 0:HB, :],
                                      cmb_[:, 0:HB, :]).then_inc(smv.h, 1)
                    vector.tensor_mul(xmv[:, HB:, :], nxt[:, HB:, :],
                                      cmb_[:, HB:, :]).then_inc(smv.h, 1)
                    sv += 2
                    vector.wait_ge(smv.h, sv)
                    vector.tensor_add(xmv, xmv, tb3).then_inc(smv.h, 1)
                    vector.drain().then_inc(smx.h, 1)
                smv.bump(6)
                v_mx = smx.bump()
                self.sec(vector=f_mx)

                spj_base = spj.v
                sev_base = sev.v

                def f_kvr(tensor, v_mx=v_mx, spj_base=spj_base,
                          sev_base=sev_base):
                    tensor.wait_ge(smx.h, v_mx)
                    g = 0
                    for wsb in (wk_sb, wv_sb, wr_sb):
                        wv_ = wsb[:].rearrange("p (kt m) -> p kt m", kt=KD)
                        for mt in range(KD):
                            if g >= 2:
                                tensor.wait_ge(sev.h, sev_base + g - 1)
                            for kt in range(KD):
                                mm = tensor.matmul(
                                    kvr_ps[:, g % 2, :],
                                    wv_[:, kt, mt * 128:(mt + 1) * 128],
                                    xmv[:, kt, :],
                                    start=(kt == 0), stop=(kt == KD - 1))
                            mm.then_inc(spj.h, 1)
                            g += 1
                spj.bump(3 * KD)

                sza_base = sza.v

                def f_ev_v(vector, spj_base=spj_base, sev_base=sev_base):
                    for mt in range(KD):
                        vector.wait_ge(spj.h, spj_base + mt + 1)
                        vector.tensor_scalar_min(kv3[:, mt, :],
                                                 kvr_ps[:, mt % 2, :],
                                                 K_CLAMP).then_inc(sev.h, 1)
                sev.bump(KD)

                def f_ev_s(scalar, spj_base=spj_base, sev_base=sev_base, c=c):
                    for mt in range(KD):
                        scalar.wait_ge(sev.h, sev_base + mt + 1)
                        scalar.activation(kv3[:, mt, :], kv3[:, mt, :], AF.Exp)
                        scalar.drain()
                    for mt in range(KD):
                        g = KD + mt
                        scalar.wait_ge(spj.h, spj_base + g + 1)
                        scalar.copy(vv3[:, mt, :],
                                    kvr_ps[:, g % 2, :]).then_inc(sev.h, 1)
                    for mt in range(KD):
                        g = 2 * KD + mt
                        scalar.wait_ge(spj.h, spj_base + g + 1)
                        scalar.activation(sr3[:, mt, :], kvr_ps[:, g % 2, :],
                                          AF.Sigmoid).then_inc(sev.h, 1)
                    scalar.drain().then_inc(sev.h, 1)
                sev.bump(2 * KD + 1)
                self.sec(tensor=f_kvr, vector=f_ev_v, scalar=f_ev_s)

                def f_wkv(vector, sev_t=sev.v, c=c):
                    vector.wait_ge(sev.h, sev_t)
                    for ct in range(KD):
                        edb = self.ppc("ed", ct).broadcast_to((128, TC))
                        vector.tensor_mul(kv_t[:], kv3[:, ct, :], vv3[:, ct, :])
                        vector.drain()
                        vector.tensor_copy(ab_buf[:, 0:1], self.wkvA[:, ct:ct + 1])
                        vector.tensor_copy(bb_buf[:, 0:1], self.wkvB[:, ct:ct + 1])
                        vector.drain()
                        vector.tensor_tensor_scan(
                            ab_buf[:, 1:1 + TC], edb, kv_t[:],
                            self.wkvA[:, ct:ct + 1], ALU.mult, ALU.add)
                        vector.drain()
                        vector.tensor_tensor_scan(
                            bb_buf[:, 1:1 + TC], edb, kv3[:, ct, :],
                            self.wkvB[:, ct:ct + 1], ALU.mult, ALU.add)
                        vector.drain()
                        vector.tensor_copy(self.wkvA[:, ct:ct + 1],
                                           ab_buf[:, TC:TC + 1])
                        vector.tensor_copy(self.wkvB[:, ct:ct + 1],
                                           bb_buf[:, TC:TC + 1])
                        vector.drain()
                        vector.scalar_tensor_tensor(t1[:], kv_t[:],
                                                    self.ppc("eu", ct),
                                                    ab_buf[:, 0:TC],
                                                    ALU.mult, ALU.add)
                        vector.drain()
                        vector.scalar_tensor_tensor(t2[:], kv3[:, ct, :],
                                                    self.ppc("eu", ct),
                                                    bb_buf[:, 0:TC],
                                                    ALU.mult, ALU.add)
                        vector.drain()
                        vector.reciprocal(t2[:], t2[:])
                        vector.drain()
                        vector.tensor_mul(t1[:], t1[:], t2[:])
                        vector.drain()
                        vector.tensor_mul(zb3[:, ct, :], sr3[:, ct, :],
                                          t1[:]).then_inc(swk.h, 1)
                swk.bump(KD)
                self.sec(vector=f_wkv)

                sao_base = sao.v
                su2_base = su2.v

                def f_ao(tensor, swk_t=swk.v, sao_base=sao_base,
                         su2_base=su2_base):
                    tensor.wait_ge(swk.h, swk_t)
                    wv_ = wo_sb[:].rearrange("p (kt m) -> p kt m", kt=KD)
                    for mt in range(KD):
                        if mt >= 2:
                            tensor.wait_ge(su2.h, su2_base + mt - 1)
                        for kt in range(KD):
                            mm = tensor.matmul(
                                ao_ps[:, mt % 2, :],
                                wv_[:, kt, mt * 128:(mt + 1) * 128],
                                zb3[:, kt, :],
                                start=(kt == 0), stop=(kt == KD - 1))
                        mm.then_inc(sao.h, 1)
                sao.bump(KD)

                def f_u2(vector, sao_base=sao_base, c=c):
                    for mt in range(KD):
                        vector.wait_ge(sao.h, sao_base + mt + 1)
                        vector.tensor_add(u1v[:, mt, 1:1 + TC],
                                          u1v[:, mt, 1:1 + TC],
                                          ao_ps[:, mt % 2, :]).then_inc(su2.h, 1)
                su2.bump(KD)
                self.sec(tensor=f_ao, vector=f_u2)

                def f_st(sync, t0=t0, su2_t=su2.v):
                    sync.wait_ge(su2.h, su2_t)
                    sync.dma_start(out=u2v[:, :, t0:t0 + TC],
                                   in_=u1v[:, :, 1:1 + TC]).then_inc(sst.h, 16)
                sst.bump(16)
                self.sec(sync=f_st)

            def f_wait(sync):
                sync.wait_ge(sst.h, sst.v)
            self.sec(sync=f_wait)

    def rwkv2(self, src, dst):
        self.begin_phase()
        NCH = self.NCH
        TW = TC + 2
        tg = "r2"
        with contextlib.ExitStack() as pst:
            sbp = lambda n, s, d=F32: self.sb(tg + n, s, d, pst)
            u3 = sbp("u3", [128, KD * TW])
            car = sbp("car", [128, KD])
            lnb = self.mk_ln_bufs(pst, tg + "l", TC)
            xm_b = sbp("xm", [128, KD * TC], BF16)
            fr_sb = sbp("fr", [128, KD * D], BF16)
            fk_s = [sbp(f"fk{i}", [128, KD * 512], BF16) for i in range(3)]
            fv_s = [sbp(f"fv{i}", [128, 4 * 512], BF16) for i in range(3)]
            kk_b = sbp("kk", [128, KF * TC], BF16)
            sf_sb = sbp("sf", [128, KD * TC], BF16)
            x2r = sbp("x2", [128, KD * TC])
            rt = [sbp(f"rt{i}", [128, TC], BF16) for i in range(2)]
            t1 = sbp("t1", [128, TC])
            tb1 = sbp("tb", [128, KD * TC], BF16)
            out_sb = sbp("o", [128, KD * TC])

            h_ps = self.ps(tg + "hp", [128, 2, TC], pst)
            y_ps = self.ps(tg + "yp", [128, 2, TC], pst)

            wl = self.gsem("sw")
            wk1 = [self.gsem("sw") for i in range(3)]
            wv1 = [self.gsem("sw") for i in range(3)]
            xl = self.gsem("hw")
            x2l = self.gsem("hw")
            smx = self.gsem("c")
            smv = self.gsem("c")
            shp = self.gsem("c")
            she = self.gsem("c")
            skk = self.gsem("c")
            syp = self.gsem("c")
            sfc = self.gsem("c")
            sfr = self.gsem("c")
            sfe = self.gsem("c")
            sou = self.gsem("c")
            sst = self.gsem("hw")
            shp_b = shp.v
            syp_b = syp.v
            sfc_b = sfc.v
            wk1_b = [s.v for s in wk1]
            wv1_b = [s.v for s in wv1]

            def f_w(gpsimd):
                gpsimd.wait_ge(self.ph.h, self.ph.v)
                gpsimd.dma_start(
                    out=fr_sb[:].rearrange("p (kt m) -> p kt m", kt=KD),
                    in_=self.FrT.rearrange("(kt p) m -> p kt m", p=128),
                ).then_inc(wl.h, 16)
                fkv = self.FkT.rearrange("(kt p) m -> p kt m", p=128)
                fvv = self.FvT.rearrange("(g kt p) m -> g p kt m", p=128, kt=4)
                for c in range(NCH):
                    for fb in range(KF // 4):
                        idx = c * (KF // 4) + fb
                        slot = idx % 3
                        if idx >= 3:
                            gpsimd.wait_ge(shp.h, shp_b + 4 * (idx - 2))
                        gpsimd.dma_start(
                            out=fk_s[slot][:].rearrange("p (kt m) -> p kt m", kt=KD),
                            in_=fkv[:, :, fb * 512:(fb + 1) * 512],
                        ).then_inc(wk1[slot].h, 16)
                        wk1[slot].bump(16)
                    for half in range(2):
                        for pair in range(2):
                            for g in range(8):
                                idx = ((c * 2 + half) * 2 + pair) * 8 + g
                                slot = idx % 3
                                if idx >= 3:
                                    pidx = idx - 3
                                    if pidx % 8 == 7:
                                        gpsimd.wait_ge(syp.h, syp_b +
                                                       pidx // 8 + 1)
                                    else:
                                        gpsimd.wait_ge(sfc.h, sfc_b +
                                                       pidx + 1 - pidx // 8)
                                gpsimd.dma_start(
                                    out=fv_s[slot][:].rearrange(
                                        "p (kt m) -> p kt m", kt=4),
                                    in_=fvv[g, :, :, half * 512:(half + 1) * 512],
                                ).then_inc(wv1[slot].h, 16)
                                wv1[slot].bump(16)
            wl.bump(16)
            for i in range(3):
                pass
            self.sec(gpsimd=f_w)

            srcv = src.rearrange("(kt p) t -> p kt t", p=128)
            dstv = dst.rearrange("(kt p) t -> p kt t", p=128)
            u2v = self.s_u2.rearrange("(kt p) t -> p kt t", p=128)
            u3v = u3[:].rearrange("p (k t) -> p k t", k=KD)
            xmv = xm_b[:].rearrange("p (k t) -> p k t", k=KD)
            kk3 = kk_b[:].rearrange("p (k t) -> p k t", k=KF)
            sf3 = sf_sb[:].rearrange("p (k t) -> p k t", k=KD)
            x2r3 = x2r[:].rearrange("p (k t) -> p k t", k=KD)
            out3 = out_sb[:].rearrange("p (k t) -> p k t", k=KD)

            nfk = [0, 0, 0]
            nfv = [0, 0, 0]
            for c in range(NCH):
                t0 = c * TC

                def f_ld(sync, c=c, t0=t0, sst_t=sst.v):
                    if c >= 1:
                        sync.wait_ge(sst.h, sst_t)
                    lo = max(t0 - 1, 0)
                    hi = min(t0 + TC + 1, self.T)
                    off = 1 if c == 0 else 0
                    sync.dma_start(
                        out=u3v[:, :, off:off + (hi - lo)],
                        in_=u2v[:, :, lo:hi]).then_inc(xl.h, 16)
                    sync.dma_start(out=x2r3, in_=srcv[:, :, t0:t0 + TC]
                                   ).then_inc(x2l.h, 16)
                v_x = xl.bump(16)
                v_x2 = x2l.bump(16)
                self.sec(sync=f_ld)

                def f_gate(eng, v=v_x, c=c, smx_t=smx.v):
                    eng.wait_ge(xl.h, v)
                    eng.wait_ge(wl.h, wl.v)
                    if c >= 1:
                        eng.wait_ge(smx.h, smx_t)
                self.sec(tensor=f_gate, vector=f_gate, scalar=f_gate)

                def f_car(vector, c=c):
                    if c == 0:
                        vector.memset(u3v[:, :, 0:1], 0.0)
                    else:
                        vector.tensor_copy(u3v[:, :, 0:1], car[:].unsqueeze(2))
                    vector.drain()
                self.sec(vector=f_car)

                ctr = lambda a: a[:, :, 1:1 + TC]
                v_u3 = self.emit_ln(lnb, ctr(u3v), TC, out_f32=ctr(u3v),
                                    g=self.ppr("ln2g"), b=self.ppr("ln2b"))
                if c < NCH - 1:
                    hb = self._halo_bufs(lnb)
                    v_u3 = self.emit_ln(hb, u3v[:, :, TW - 1:TW], 1,
                                        out_f32=u3v[:, :, TW - 1:TW],
                                        g=self.ppr("ln2g"), b=self.ppr("ln2b"),
                                        wait_sv=v_u3)
                else:
                    def f_hz(vector):
                        vector.memset(u3v[:, :, TW - 1:TW], 0.0)
                        vector.drain()
                    self.sec(vector=f_hz)

                def f_sc(vector, v_u3=v_u3):
                    vector.tensor_copy(car[:].unsqueeze(2), u3v[:, :, TC:TC + 1])
                    vector.drain()
                self.sec(vector=f_sc)

                def f_mx(vector, smv_base=smv.v):
                    sv = smv_base
                    cur = u3v[:, :, 1:1 + TC]
                    prv = u3v[:, :, 0:TC]
                    nxt = u3v[:, :, 2:2 + TC]
                    HB = KD // 2
                    bc = lambda n: self.ppr(n).unsqueeze(2).broadcast_to(
                        (128, KD, TC))
                    tb3 = tb1[:].rearrange("p (k t) -> p k t", k=KD)
                    vector.tensor_mul(tb3, cur, bc("ftmix")).then_inc(smv.h, 1)
                    vector.tensor_mul(xmv, prv, bc("ftmix1m")).then_inc(smv.h, 1)
                    sv += 2
                    vector.wait_ge(smv.h, sv)
                    vector.tensor_add(tb3, tb3, xmv).then_inc(smv.h, 1)
                    sv += 1
                    vector.wait_ge(smv.h, sv)
                    cmb_ = bc("fcmix")
                    vector.tensor_mul(xmv[:, 0:HB, :], prv[:, 0:HB, :],
                                      cmb_[:, 0:HB, :]).then_inc(smv.h, 1)
                    vector.tensor_mul(xmv[:, HB:, :], nxt[:, HB:, :],
                                      cmb_[:, HB:, :]).then_inc(smv.h, 1)
                    sv += 2
                    vector.wait_ge(smv.h, sv)
                    vector.tensor_add(xmv, xmv, tb3).then_inc(smv.h, 1)
                    vector.drain().then_inc(smx.h, 1)
                smv.bump(6)
                v_mx = smx.bump()
                self.sec(vector=f_mx)

                sfr_base = sfr.v
                sfe_base = sfe.v

                def f_fr(tensor, v_mx=v_mx, sfr_base=sfr_base,
                         sfe_base=sfe_base, c=c, she_t=she.v):
                    tensor.wait_ge(smx.h, v_mx)
                    if c >= 1:
                        tensor.wait_ge(she.h, she_t)
                    wv_ = fr_sb[:].rearrange("p (kt m) -> p kt m", kt=KD)
                    for mt in range(KD):
                        if mt >= 2:
                            tensor.wait_ge(sfe.h, sfe_base + mt - 1)
                        for kt in range(KD):
                            mm = tensor.matmul(
                                h_ps[:, mt % 2, :],
                                wv_[:, kt, mt * 128:(mt + 1) * 128],
                                xmv[:, kt, :],
                                start=(kt == 0), stop=(kt == KD - 1))
                        mm.then_inc(sfr.h, 1)
                sfr.bump(KD)

                def f_fre(scalar, sfr_base=sfr_base):
                    for mt in range(KD):
                        scalar.wait_ge(sfr.h, sfr_base + mt + 1)
                        scalar.activation(sf3[:, mt, :], h_ps[:, mt % 2, :],
                                          AF.Sigmoid)
                        scalar.drain().then_inc(sfe.h, 1)
                sfe.bump(KD)
                self.sec(tensor=f_fr, scalar=f_fre)

                shp_base = shp.v
                she_base = she.v
                skk_base = skk.v

                def f_fk(tensor, c=c, v_mx=v_mx, shp_base=shp_base,
                         she_base=she_base, sfe_t=sfe.v):
                    tensor.wait_ge(sfe.h, sfe_t)
                    for fb in range(KF // 4):
                        idx = c * (KF // 4) + fb
                        slot = idx % 3
                        nfk[slot] += 1
                        tensor.wait_ge(wk1[slot].h, wk1_b[slot] + nfk[slot] * 16)
                        wv_ = fk_s[slot][:].rearrange("p (kt m) -> p kt m", kt=KD)
                        for ft in range(4):
                            gg = fb * 4 + ft
                            if gg >= 2:
                                tensor.wait_ge(she.h, she_base + gg - 1)
                            for kt in range(KD):
                                mm = tensor.matmul(
                                    h_ps[:, gg % 2, :],
                                    wv_[:, kt, ft * 128:(ft + 1) * 128],
                                    xmv[:, kt, :],
                                    start=(kt == 0), stop=(kt == KD - 1))
                            mm.then_inc(shp.h, 1)
                shp.bump(KF)

                def f_fke(scalar, shp_base=shp_base, skk_base=skk_base):
                    for gg in range(KF):
                        scalar.wait_ge(shp.h, shp_base + gg + 1)
                        scalar.activation(rt[gg % 2][:], h_ps[:, gg % 2, :],
                                          AF.Relu).then_inc(she.h, 1)
                        scalar.drain()
                        scalar.activation(kk3[:, gg, :], rt[gg % 2][:],
                                          AF.Square).then_inc(skk.h, 1)
                she.bump(KF)
                skk.bump(KF)
                self.sec(tensor=f_fk, scalar=f_fke)

                syp_base = syp.v
                sou_base = sou.v

                def f_fv(tensor, c=c, skk_t=skk.v, syp_base=syp_base,
                         sou_base=sou_base):
                    tensor.wait_ge(skk.h, skk_t)
                    for half in range(2):
                        for pair in range(2):
                            pp_ = (c * 2 + half) * 2 + pair
                            if half * 2 + pair >= 1:
                                tensor.wait_ge(sou.h,
                                               sou_base + (half * 2 + pair) * 2)
                            elif c >= 1:
                                tensor.wait_ge(sou.h, sou_base)
                            for g in range(8):
                                idx = pp_ * 8 + g
                                slot = idx % 3
                                nfv[slot] += 1
                                tensor.wait_ge(wv1[slot].h, wv1_b[slot] + nfv[slot] * 16)
                                wv_ = fv_s[slot][:].rearrange(
                                    "p (kt m) -> p kt m", kt=4)
                                for kt in range(4):
                                    gkt = g * 4 + kt
                                    for mi2 in range(2):
                                        mloc = pair * 2 + mi2
                                        mm = tensor.matmul(
                                            y_ps[:, mi2, :],
                                            wv_[:, kt, mloc * 128:(mloc + 1) * 128],
                                            kk3[:, gkt, :],
                                            start=(gkt == 0),
                                            stop=(gkt == KF - 1))
                                    if gkt == KF - 1:
                                        mm.then_inc(syp.h, 1)
                                if g < 7:
                                    mm.then_inc(sfc.h, 1)
                syp.bump(4)
                sfc.bump(28)

                def f_ou(vector, c=c, syp_base=syp_base, sfe_t=sfe.v,
                         v_x2=v_x2, sou_base=sou_base):
                    vector.wait_ge(sfe.h, sfe_t)
                    vector.wait_ge(x2l.h, v_x2)
                    for mt in range(KD):
                        vector.wait_ge(syp.h, syp_base + mt // 2 + 1)
                        if mt >= 1:
                            vector.drain()
                        vector.tensor_mul(t1[:], y_ps[:, mt % 2, :],
                                          sf3[:, mt, :])
                        vector.drain()
                        vector.tensor_add(t1[:], t1[:], u3v[:, mt, 1:1 + TC])
                        vector.drain()
                        vector.tensor_add(out3[:, mt, :], t1[:],
                                          x2r3[:, mt, :]).then_inc(sou.h, 1)
                sou.bump(KD)
                self.sec(tensor=f_fv, vector=f_ou)

                def f_st(sync, t0=t0, sou_t=sou.v):
                    sync.wait_ge(sou.h, sou_t)
                    sync.dma_start(out=dstv[:, :, t0:t0 + TC], in_=out3
                                   ).then_inc(sst.h, 16)
                sst.bump(16)
                self.sec(sync=f_st)

            def f_wait(sync):
                sync.wait_ge(sst.h, sst.v)
            self.sec(sync=f_wait)



def _pp(v, k):
    return np.ascontiguousarray(np.asarray(v, np.float32).reshape(k, 128).T)


def prep_shared(i):
    import ml_dtypes
    f32 = np.float32

    def t(a):
        return np.ascontiguousarray(np.asarray(a, f32).T)

    out = {
        "xpT": t(i["L0_attn_a_x_proj_w"]),
        "dtpT": t(i["L0_attn_a_dt_proj_w"]),
        "bcT": np.ascontiguousarray(np.concatenate(
            [np.asarray(i["L0_attn_a_B_proj_w"], f32),
             np.asarray(i["L0_attn_a_C_proj_w"], f32)], 0).T),
        "opT": t(i["L0_attn_a_out_proj_w"]),
        "A_log_pp": np.ascontiguousarray(
            np.asarray(i["L0_attn_a_A_log"], f32).reshape(KI, 128, NS)
            .transpose(1, 0, 2).reshape(128, KI * NS)),
        "dtb_pp": _pp(i["L0_attn_a_dt_proj_b"], KI),
        "convw_pp": np.ascontiguousarray(
            np.asarray(i["L0_attn_a_conv_w"], f32).reshape(DI, 4)
            .reshape(KI, 128, 4).transpose(1, 0, 2).reshape(128, KI * 4)),
        "convb_pp": _pp(i["L0_attn_a_conv_b"], KI),
        "D_pp": _pp(i["L0_attn_a_D"], KI),
        "WkT": t(i["L0_attn_b_Wk"]), "WvT": t(i["L0_attn_b_Wv"]),
        "WrT": t(i["L0_attn_b_Wr"]), "WoT": t(i["L0_attn_b_Wo"]),
        "FkT": t(i["L0_attn_b_Fk"]), "FvT": t(i["L0_attn_b_Fv"]),
        "FrT": t(i["L0_attn_b_Fr"]),
        "ln1g_pp": _pp(i["L0_attn_b_ln1_g"], KD),
        "ln1b_pp": _pp(i["L0_attn_b_ln1_b"], KD),
        "ln2g_pp": _pp(i["L0_attn_b_ln2_g"], KD),
        "ln2b_pp": _pp(i["L0_attn_b_ln2_b"], KD),
        "tmix_pp": _pp(np.asarray(i["L0_attn_b_att_time_mix"]).reshape(D), KD),
        "cmix_pp": _pp(np.asarray(i["L0_attn_b_att_combined_mix"]).reshape(D), KD),
        "ftmix_pp": _pp(np.asarray(i["L0_attn_b_ffn_time_mix"]).reshape(D), KD),
        "fcmix_pp": _pp(np.asarray(i["L0_attn_b_ffn_combined_mix"]).reshape(D), KD),
        "tdec_pp": _pp(i["L0_attn_b_time_decay"], KD),
        "tfir_pp": _pp(i["L0_attn_b_time_first"], KD),
        "mixw": np.ascontiguousarray(np.asarray(i["L0_ffn_mix_w"], f32)),
        "mixb_pp": _pp(i["L0_ffn_mix_b"], KD),
        "g1_pp": _pp(i["L0_ffn_g1"], KD), "be1_pp": _pp(i["L0_ffn_be1"], KD),
        "g2_pp": _pp(i["L0_ffn_g2"], KD), "be2_pp": _pp(i["L0_ffn_be2"], KD),
        "routerw": np.ascontiguousarray(np.asarray(i["L0_ffn_router_w"], f32)),
        "w1": np.ascontiguousarray(np.asarray(i["L0_ffn_w1"], f32)),
        "b1_pp": np.ascontiguousarray(
            np.asarray(i["L0_ffn_b1"], f32).reshape(NE, KF, 128)
            .transpose(2, 0, 1).reshape(128, NE * KF)),
        "w2": np.ascontiguousarray(np.asarray(i["L0_ffn_w2"], f32)),
        "b2": np.ascontiguousarray(np.asarray(i["L0_ffn_b2"], f32)),
        "eye32": np.eye(128, dtype=f32),
        "ones1": np.ones((1, 128), f32),
        "ones1b": np.ones((128, 1), ml_dtypes.bfloat16),
    }
    sel16 = np.zeros((NS, NS * 128), ml_dtypes.bfloat16)
    for s in range(NS):
        sel16[s, s * 128:(s + 1) * 128] = 1.0
    out["sel16"] = sel16
    sel8 = np.zeros((NE, NE * 128), f32)
    for e in range(NE):
        sel8[e, e * 128:(e + 1) * 128] = 1.0
    out["sel8"] = sel8
    return out


def kernel(**inputs):
    x = np.asarray(inputs["L0_attn_a_x"], np.float32)
    B, T, _ = x.shape
    shared = prep_shared(inputs)
    nc = KB(T=T).build()
    in_maps = []
    for b in range(B):
        m = dict(shared)
        m["xT"] = np.ascontiguousarray(x[b].T)
        in_maps.append(m)
    res = run_bass_kernel_spmd(nc, in_maps, core_ids=list(range(B)))
    return np.stack([np.ascontiguousarray(r["yout"].T) for r in res.results], 0)


# version 16
# speedup vs baseline: 1.0565x; 1.0565x over previous
import contextlib
import numpy as np

import concourse.bass as bass
import concourse.mybir as mybir
from concourse.bass_utils import run_bass_kernel_spmd

F32 = mybir.dt.float32
BF16 = mybir.dt.bfloat16
AF = mybir.ActivationFunctionType
ALU = mybir.AluOpType
AX = mybir.AxisListType

D = 1024
DI = 2048
NS = 16
FF = 4096
NE = 8
KD = D // 128
KI = DI // 128
KF = FF // 128
LN_EPS = 1e-5
K_CLAMP = 60.0
TC = 512


class Sem:
    __slots__ = ("h", "v")

    def __init__(self, h):
        self.h = h
        self.v = 0

    def bump(self, k=1):
        self.v += k
        return self.v


class KB:

    def __init__(self, T=4096, n_phase=99):
        self.T = T
        self.NCH = T // TC
        self.n_phase = n_phase
        self.nc = bass.Bass("TRN2", target_bir_lowering=False, debug=False)
        self._semn = 0
        self._pools = {}
        self._cursor = {"c": 0, "hw": 0, "sw": 0}

    def sem(self, name=None):
        self._semn += 1
        h = self.st.enter_context(self.nc.semaphore(name or f"s{self._semn}"))
        return Sem(h)

    def begin_phase(self):
        self._cursor = {"c": 0, "hw": 0, "sw": 0}

    def gsem(self, kind="c", name=None):
        cur = self._cursor[kind]
        pool = self._pools.setdefault(kind, [])
        if cur == len(pool):
            pool.append(Sem(self.st.enter_context(
                self.nc.semaphore(f"{kind}{cur}"))))
        self._cursor[kind] += 1
        return pool[cur]

    def sb(self, name, shape, dt=F32, stack=None):
        return (stack or self.st).enter_context(
            self.nc.sbuf_tensor(name, list(shape), dt))

    def ps(self, name, shape, stack=None):
        return (stack or self.st).enter_context(
            self.nc.psum_tensor(name, list(shape), F32))

    def sec(self, **fns):
        order = ("sync", "tensor", "vector", "scalar", "gpsimd")
        for name in order:
            if name in fns:
                getattr(self.block, name)(fns[name])

    def barrier(self):
        ph = self.ph
        target = ph.v + 5

        def mk(e):
            def f(eng):
                if e in ("vector", "scalar", "gpsimd"):
                    eng.drain().then_inc(ph.h, 1)
                else:
                    eng.nop().then_inc(ph.h, 1)
                eng.wait_ge(ph.h, target)
            return f
        self.sec(**{e: mk(e) for e in
                    ("sync", "tensor", "vector", "scalar", "gpsimd")})
        ph.v = target

    def build(self):
        nc = self.nc
        T = self.T

        def din(name, shape, dt=F32):
            return nc.dram_tensor(name, list(shape), dt, kind="ExternalInput").ap()

        def dint(name, shape, dt=F32):
            return nc.dram_tensor(name, list(shape), dt, kind="Internal").ap()

        self.xT = din("xT", [D, T])
        self.xpT = din("xpT", [D, DI]); self.dtpT = din("dtpT", [D, DI])
        self.bcT = din("bcT", [D, 2 * NS]); self.opT = din("opT", [DI, D])
        self.A_log_pp = din("A_log_pp", [128, KI * NS])
        self.dtb_pp = din("dtb_pp", [128, KI])
        self.convw_pp = din("convw_pp", [128, KI * 4])
        self.convb_pp = din("convb_pp", [128, KI])
        self.D_pp = din("D_pp", [128, KI])
        self.WkT = din("WkT", [D, D]); self.WvT = din("WvT", [D, D])
        self.WrT = din("WrT", [D, D]); self.WoT = din("WoT", [D, D])
        self.FkT = din("FkT", [D, FF]); self.FvT = din("FvT", [FF, D])
        self.FrT = din("FrT", [D, D])
        for n in ("ln1g", "ln1b", "ln2g", "ln2b", "tmix", "cmix", "ftmix",
                  "fcmix", "tdec", "tfir", "mixb", "g1", "be1", "g2", "be2"):
            setattr(self, n + "_d", din(n + "_pp", [128, KD]))
        self.mixw = din("mixw", [D, D])
        self.routerw = din("routerw", [D, NE])
        self.w1 = din("w1", [NE, D, FF])
        self.b1_pp = din("b1_pp", [128, NE * KF])
        self.w2 = din("w2", [NE, FF, D])
        self.b2 = din("b2", [NE, D])
        self.eye32 = din("eye32", [128, 128])
        self.ones1 = din("ones1", [1, 128])
        self.ones1b = din("ones1b", [128, 1], BF16)
        self.sel16 = din("sel16", [NS, NS * 128], BF16)
        self.sel8 = din("sel8", [NE, NE * 128])

        self.yout = nc.dram_tensor("yout", [D, T], F32, kind="ExternalOutput").ap()
        self.x_work = dint("x_work", [D, T])
        self.s_dt = dint("s_dt", [DI, T], BF16)
        self.s_xc = dint("s_xc", [DI, T], BF16)
        self.s_bc = dint("s_bc", [2 * NS, T], BF16)
        self.s_n2 = dint("s_n2", [D, T], BF16)
        self.s_u2 = dint("s_u2", [D, T])
        self.s_ya = dint("s_ya", [D, T])
        self.s_cb = dint("s_cb", [NE, T], BF16)

        self.st = contextlib.ExitStack()
        with self.st:
            self._persistent()
            self.block = self.st.enter_context(nc.Block())
            self.ph = self.sem("ph")
            self.p0()
            self.barrier()
            if self.n_phase >= 1:
                self.mamba1()
                self.barrier()
            if self.n_phase >= 2:
                self.mamba2(src=self.xT, dst=self.x_work)
                self.barrier()
            if self.n_phase >= 3:
                self.ffn_pre(src=self.x_work, call=0)
                self.barrier()
            if self.n_phase >= 4:
                last = (self.n_phase == 4)
                self.ffn_exp(dst=(self.yout if last else self.x_work), call=0)
                self.barrier()
            if self.n_phase >= 5:
                self.rwkv1(src=self.x_work)
                self.barrier()
            if self.n_phase >= 6:
                self.rwkv2(src=self.x_work, dst=self.x_work)
                self.barrier()
            if self.n_phase >= 7:
                self.ffn_pre(src=self.x_work, call=1)
                self.barrier()
            if self.n_phase >= 8:
                self.ffn_exp(dst=self.yout, call=1)
                self.barrier()
            if self.n_phase in (2, 3, 5, 6, 7):
                eps = self.sem("ep")

                def f_ep(sync):
                    sync.dma_start(out=self.yout, in_=self.x_work).then_inc(eps.h, 16)
                    sync.wait_ge(eps.h, 16)
                eps.bump(16)
                self.sec(sync=f_ep)
            elif self.n_phase < 2:
                eps = self.sem("ep")

                def f_ep(sync):
                    sync.dma_start(out=self.yout, in_=self.xT).then_inc(eps.h, 16)
                    sync.wait_ge(eps.h, 16)
                self.sec(sync=f_ep)
        return nc

    def _persistent(self):
        sb = self.sb
        ppcols = [
            ("A", KI * NS), ("dtb", KI), ("convw", KI * 4), ("convb", KI),
            ("Dm", KI), ("ln1g", KD), ("ln1b", KD), ("ln2g", KD), ("ln2b", KD),
            ("tmix", KD), ("tmix1m", KD), ("cmix", KD),
            ("ftmix", KD), ("ftmix1m", KD), ("fcmix", KD),
            ("ed", KD), ("eu", KD),
            ("mixb", KD), ("g1", KD), ("be1", KD), ("g2", KD), ("be2", KD),
            ("b1", NE * KF), ("c_m1", 1), ("c_mh", 1),
        ]
        self.PP = {}
        off = 0
        for n, w in ppcols:
            self.PP[n] = (off, w)
            off += w
        self.pp = sb("pp", [128, off])
        self.ones_sb = sb("ones_sb", [1, 128])
        self.onesb_sb = sb("onesb_sb", [128, 1], BF16)
        self.mst = sb("mst", [128, KI * NS])
        self.wkvA = sb("wkvA", [128, KD])
        self.wkvB = sb("wkvB", [128, KD])

    def ppc(self, name, i=0):
        o, w = self.PP[name]
        return self.pp[:, o + i:o + i + 1]

    def ppr(self, name):
        o, w = self.PP[name]
        return self.pp[:, o:o + w]

    def p0(self):
        p0 = self.sem("p0")
        pairs = [
            (self.ppr("A"), self.A_log_pp), (self.ppr("dtb"), self.dtb_pp),
            (self.ppr("convw"), self.convw_pp),
            (self.ppr("convb"), self.convb_pp), (self.ppr("Dm"), self.D_pp),
            (self.ppr("mixb"), self.mixb_d), (self.ppr("b1"), self.b1_pp),
            (self.ones_sb[:], self.ones1), (self.onesb_sb[:], self.ones1b),
        ]
        for n in ("ln1g", "ln1b", "ln2g", "ln2b", "tmix", "cmix", "ftmix",
                  "fcmix", "g1", "be1", "g2", "be2"):
            pairs.append((self.ppr(n), getattr(self, n + "_d")))
        pairs.append((self.ppr("ed"), self.tdec_d))
        pairs.append((self.ppr("eu"), self.tfir_d))

        def f_sync(sync):
            for dst, src in pairs:
                sync.dma_start(out=dst, in_=src).then_inc(p0.h, 16)
        p0.bump(16 * len(pairs))

        ps0 = self.sem("p0s")
        pv0 = self.sem("p0v")

        def f_vec0(vector):
            vector.memset(self.ppr("c_m1"), -1.0)
            vector.memset(self.ppr("c_mh"), -0.5)
            vector.drain().then_inc(pv0.h, 1)
        pv0.bump()

        def f_scalar(scalar):
            scalar.wait_ge(p0.h, p0.v)
            scalar.wait_ge(pv0.h, pv0.v)
            scalar.activation(self.ppr("A"), self.ppr("A"), AF.Exp)
            scalar.drain()
            scalar.activation(self.ppr("ed"), self.ppr("ed"), AF.Exp)
            scalar.drain()
            scalar.activation(self.ppr("ed"), self.ppr("ed"), AF.Exp,
                              scale=self.ppc("c_m1"))
            scalar.drain()
            scalar.activation(self.ppr("eu"), self.ppr("eu"), AF.Exp)
            scalar.drain().then_inc(ps0.h, 1)
        ps0.bump()

        def f_vector(vector):
            vector.wait_ge(ps0.h, ps0.v)
            vector.tensor_scalar_mul(self.ppr("A"), self.ppr("A"), -1.0)
            vector.drain()
            vector.tensor_scalar(self.ppr("tmix1m"), self.ppr("tmix"), -1.0,
                                 1.0, ALU.mult, ALU.add)
            vector.drain()
            vector.tensor_scalar(self.ppr("ftmix1m"), self.ppr("ftmix"), -1.0,
                                 1.0, ALU.mult, ALU.add)
            vector.drain()
            vector.memset(self.mst[:], 0.0)
            vector.memset(self.wkvA[:], 0.0)
            vector.memset(self.wkvB[:], 0.0)
            vector.drain()
        self.sec(sync=f_sync, vector=f_vec0)
        self.sec(scalar=f_scalar)
        self.sec(vector=f_vector)

    def emit_ln(self, bufs, x3, n, out_f32=None, out_b16=None, g=None, b=None,
                wait_sv=0):
        sv, ss, sp = bufs["sv"], bufs["ss"], bufs["sp"]
        xb, sqb = bufs["xb"], bufs["sqb"]
        rows_sb, lnt = bufs["rows_sb"], bufs["lnt"]
        rows_ps, bc_ps = bufs["rows_ps"], bufs["bc_ps"]

        def fv1(vector):
            vector.tensor_copy(xb, x3).then_inc(sv.h, 1)
        v_xb = sv.bump()

        def fs1(scalar):
            if wait_sv:
                scalar.wait_ge(sv.h, wait_sv)
            scalar.activation(sqb, x3, AF.Square).then_inc(ss.h, 1)
        s_sq = ss.bump()

        def ft1(tensor):
            tensor.wait_ge(sv.h, v_xb)
            for kt in range(KD):
                mm = tensor.matmul(rows_ps[:, 0:n], self.onesb_sb[:],
                                   xb[:, kt, :], start=(kt == 0),
                                   stop=(kt == KD - 1))
            mm.then_inc(sp.h, 1)
            tensor.wait_ge(ss.h, s_sq)
            for kt in range(KD):
                mm = tensor.matmul(rows_ps[:, n:2 * n], self.onesb_sb[:],
                                   sqb[:, kt, :], start=(kt == 0),
                                   stop=(kt == KD - 1))
            mm.then_inc(sp.h, 1)
        p_rows = sp.bump(2)

        def fv2(vector):
            vector.wait_ge(sp.h, p_rows)
            vector.tensor_scalar_mul(rows_sb[:, 0:2 * n], rows_ps[:, 0:2 * n],
                                     1.0 / D)
            vector.drain()
            vector.tensor_mul(rows_sb[:, 2 * n:3 * n], rows_sb[:, 0:n],
                              rows_sb[:, 0:n])
            vector.drain()
            vector.tensor_sub(rows_sb[:, n:2 * n], rows_sb[:, n:2 * n],
                              rows_sb[:, 2 * n:3 * n])
            vector.drain()
            vector.tensor_scalar_add(rows_sb[:, n:2 * n], rows_sb[:, n:2 * n],
                                     LN_EPS).then_inc(sv.h, 1)
        v_var = sv.bump()

        def fs2(scalar):
            scalar.wait_ge(sv.h, v_var)
            scalar.activation(rows_sb[:, n:2 * n], rows_sb[:, n:2 * n], AF.Ln)
            scalar.drain()
            scalar.activation(rows_sb[:, n:2 * n], rows_sb[:, n:2 * n], AF.Exp,
                              scale=self.ppc("c_mh")[:1, :])
            scalar.drain().then_inc(ss.h, 1)
        s_rstd = ss.bump()

        def fv3(vector):
            vector.wait_ge(ss.h, s_rstd)
            vector.tensor_mul(rows_sb[:, 0:n], rows_sb[:, 0:n],
                              rows_sb[:, n:2 * n]).then_inc(sv.h, 1)
        v_mrs = sv.bump()

        def ft2(tensor):
            tensor.wait_ge(sv.h, v_mrs)
            mm = tensor.matmul(bc_ps[:, 0, :n], self.ones_sb[:],
                               rows_sb[:, n:2 * n], start=True, stop=True)
            mm.then_inc(sp.h, 1)
            mm = tensor.matmul(bc_ps[:, 1, :n], self.ones_sb[:],
                               rows_sb[:, 0:n], start=True, stop=True)
            mm.then_inc(sp.h, 1)
        p_bc = sp.bump(2)

        def fv4(vector):
            vector.wait_ge(sp.h, p_bc)
            kd = x3.shape[1]
            rb = bc_ps[:, 0:1, :n].broadcast_to((128, kd, n))
            mb = bc_ps[:, 1:2, :n].broadcast_to((128, kd, n))
            vector.tensor_mul(lnt, x3, rb)
            vector.drain()
            vector.tensor_sub(lnt, lnt, mb)
            vector.drain()
            if g is not None:
                for kt in range(KD):
                    vector.tensor_scalar(lnt[:, kt, :], lnt[:, kt, :],
                                         g[:, kt:kt + 1], b[:, kt:kt + 1],
                                         ALU.mult, ALU.add)
                vector.drain()
            if out_f32 is not None:
                vector.tensor_copy(out_f32, lnt)
                vector.drain()
            if out_b16 is not None:
                vector.tensor_copy(out_b16, lnt)
                vector.drain()
            vector.drain().then_inc(sv.h, 1)
        v_done = sv.bump()

        self.sec(vector=fv1, scalar=fs1)
        self.sec(tensor=ft1)
        self.sec(vector=fv2)
        self.sec(scalar=fs2)
        self.sec(vector=fv3)
        self.sec(tensor=ft2)
        self.sec(vector=fv4)
        return v_done

    def _halo_bufs(self, lnb):
        return {
            "xb": lnb["xb"][:, :, 0:1], "sqb": lnb["sqb"][:, :, 0:1],
            "rows_sb": lnb["rows_sb"][:, 0:3], "lnt": lnb["lnt"][:, :, 0:1],
            "rows_ps": lnb["rows_ps"][:, 0:2], "bc_ps": lnb["bc_ps"][:, :, 0:1],
            "sv": lnb["sv"], "ss": lnb["ss"], "sp": lnb["sp"],
        }

    def mk_ln_bufs(self, pst, tag, n):
        return {
            "xb": self.sb(tag + "xb", [128, KD * n], BF16, pst)[:]
                .rearrange("p (k t) -> p k t", k=KD),
            "sqb": self.sb(tag + "sq", [128, KD * n], BF16, pst)[:]
                .rearrange("p (k t) -> p k t", k=KD),
            "rows_sb": self.sb(tag + "rw", [1, 3 * n], F32, pst)[:],
            "lnt": self.sb(tag + "ln", [128, KD * n], F32, pst)[:]
                .rearrange("p (k t) -> p k t", k=KD),
            "rows_ps": self.ps(tag + "rp", [1, 2 * n], pst)[:],
            "bc_ps": self.ps(tag + "bp", [128, 2, n], pst)[:],
            "sv": self.gsem("c"), "ss": self.gsem("c"),
            "sp": self.gsem("c"),
        }

    def mamba1(self):
        self.begin_phase()
        NCH = self.NCH
        with contextlib.ExitStack() as pst:
            sbp = lambda n, s, d=F32: self.sb("m1" + n, s, d, pst)
            xsb = [sbp("x0", [128, KD * TC])]
            lnb = self.mk_ln_bufs(pst, "m1l", TC)
            xn_b = sbp("xn", [128, KD * TC], BF16)
            xpT_sb = sbp("wxp", [128, KD * DI], BF16)
            dtpT_sb = sbp("wdt", [128, KD * DI], BF16)
            bcT_sb = sbp("wbc", [128, KD * 2 * NS], BF16)
            xi = [sbp("xi0", [128, KI * (TC + 3)], BF16)]
            dt_st = [sbp("dt0", [128, KI * TC], BF16)]
            xc_st = [sbp("xc0", [128, KI * TC], BF16)]
            bc_st = [sbp("bc0", [2 * NS, TC], BF16)]
            ctmp_b = sbp("ctb", [128, KI * TC], BF16)

            proj_ps = self.ps("m1pp", [128, 2, TC], pst)
            bcp_ps = self.ps("m1cp", [2 * NS, TC], pst)

            xl = [self.gsem("hw")]
            wl = self.gsem("sw")
            spj = self.gsem("c")
            sevx = self.gsem("c")
            sevd = self.gsem("c")
            svdt = self.gsem("c")
            sact = self.gsem("c")
            scv = self.gsem("c")
            sxc = self.gsem("c")
            stl = self.gsem("c")
            sst = [self.gsem("hw")]
            sbc = self.gsem("c")
            sst_base0 = sst[0].v
            stl_base0 = stl.v

            def f_w(gpsimd):
                gpsimd.wait_ge(self.ph.h, self.ph.v)
                for dst, src, kk in ((xpT_sb, self.xpT, DI),
                                     (dtpT_sb, self.dtpT, DI),
                                     (bcT_sb, self.bcT, 2 * NS)):
                    gpsimd.dma_start(
                        out=dst[:].rearrange("p (kt m) -> p kt m", kt=KD),
                        in_=src.rearrange("(kt p) m -> p kt m", p=128),
                    ).then_inc(wl.h, 16)
            wl.bump(48)
            self.sec(gpsimd=f_w)

            xTv = self.xT.rearrange("(kt p) t -> p kt t", p=128)
            sdtv = self.s_dt.rearrange("(kt p) t -> p kt t", p=128)
            sxcv = self.s_xc.rearrange("(kt p) t -> p kt t", p=128)

            ln_done = {}
            scv_g0 = scv.v
            for c in range(NCH):
                sl = 0
                t0 = c * TC

                def f_ld(sync, t0=t0, c=c):
                    if c >= 1:
                        sync.wait_ge(lnb["sv"].h, ln_done[c - 1])
                        sync.wait_ge(scv.h, scv_g0 + 8 * c)
                    sync.dma_start(
                        out=xsb[0][:].rearrange("p (k t) -> p k t", k=KD),
                        in_=xTv[:, :, t0:t0 + TC]).then_inc(xl[0].h, 16)
                v_x = xl[0].bump(16)
                self.sec(sync=f_ld)

                spj_prev = spj.v

                def f_gate(eng, v=v_x, c=c, spj_prev=spj_prev):
                    eng.wait_ge(xl[0].h, v)
                    if c >= 1:
                        eng.wait_ge(spj.h, spj_prev)
                        eng.wait_ge(sst[0].h, sst_base0 + 48 * c)
                    eng.wait_ge(wl.h, wl.v)
                self.sec(tensor=f_gate, vector=f_gate, scalar=f_gate)

                x3 = xsb[0][:].rearrange("p (k t) -> p k t", k=KD)
                ln_done[c] = self.emit_ln(
                    lnb, x3, TC,
                    out_b16=xn_b[:].rearrange("p (k t) -> p k t", k=KD))

                xnv = xn_b[:].rearrange("p (k t) -> p k t", k=KD)
                xiv = xi[0][:].rearrange("p (k t) -> p k t", k=KI)

                def f_tail(vector, c=c):
                    if c == 0:
                        vector.memset(xiv[:, :, 0:3], 0.0)
                    else:
                        vector.tensor_copy(xiv[:, :, 0:3], xiv[:, :, TC:TC + 3])
                    vector.drain().then_inc(stl.h, 1)
                stl.bump(1)
                self.sec(vector=f_tail)

                spj_base = spj.v
                sevx_base = sevx.v
                sevd_base = sevd.v
                v_ln = ln_done[c]

                def f_proj(tensor, c=c, v_ln=v_ln, spj_base=spj_base,
                           sevx_base=sevx_base, sevd_base=sevd_base):
                    tensor.wait_ge(lnb["sv"].h, v_ln)
                    g = 0
                    for wsb in (xpT_sb, dtpT_sb):
                        wv = wsb[:].rearrange("p (kt m) -> p kt m", kt=KD)
                        for mt in range(KI):
                            bank = g % 2
                            if 2 <= g <= KI + 1:
                                tensor.wait_ge(sevx.h, sevx_base + g - 1)
                            elif g > KI + 1:
                                tensor.wait_ge(sevd.h, sevd_base + g - KI - 1)
                            for kt in range(KD):
                                mm = tensor.matmul(
                                    proj_ps[:, bank, :],
                                    wv[:, kt, mt * 128:(mt + 1) * 128],
                                    xnv[:, kt, :],
                                    start=(kt == 0), stop=(kt == KD - 1))
                            mm.then_inc(spj.h, 1)
                            g += 1
                    wv = bcT_sb[:].rearrange("p (kt m) -> p kt m", kt=KD)
                    for kt in range(KD):
                        mm = tensor.matmul(bcp_ps[:], wv[:, kt, :],
                                           xnv[:, kt, :],
                                           start=(kt == 0), stop=(kt == KD - 1))
                    mm.then_inc(spj.h, 1)
                spj.bump(2 * KI + 1)
                self.sec(tensor=f_proj)

                def f_xi(scalar, spj_base=spj_base, sl=sl, c=c):
                    scalar.wait_ge(stl.h, stl_base0 + c + 1)
                    for mt in range(KI):
                        scalar.wait_ge(spj.h, spj_base + mt + 1)
                        scalar.copy(xiv[:, mt, 3:3 + TC],
                                    proj_ps[:, mt % 2, :]).then_inc(sevx.h, 1)
                sevx.bump(KI)
                self.sec(scalar=f_xi)

                sevd_base = sevd.v
                dt_full = dt_st[0][:]
                xc_full = xc_st[0][:]
                ct_full = ctmp_b[:]

                def f_dtv(vector, spj_base=spj_base, sevd_base=sevd_base,
                          svd_base=svdt.v):
                    dtv = dt_st[0][:].rearrange("p (k t) -> p k t", k=KI)
                    for mt in range(KI):
                        g = KI + mt
                        vector.wait_ge(spj.h, spj_base + g + 1)
                        vector.tensor_scalar_add(
                            dtv[:, mt, :], proj_ps[:, g % 2, :],
                            self.ppc("dtb", mt)).then_inc(sevd.h, 1)
                    vector.wait_ge(sevd.h, sevd_base + KI)
                    vector.scalar_tensor_tensor(
                        xc_full, dt_full, -1.0, dt_full,
                        ALU.mult, ALU.max).then_inc(svdt.h, 1)
                    vector.wait_ge(svdt.h, svd_base + 1)
                    vector.tensor_scalar_max(dt_full, dt_full,
                                             0.0).then_inc(svdt.h, 1)
                sevd.bump(KI)

                def f_dts(scalar, svd_base=svdt.v):
                    scalar.wait_ge(svdt.h, svd_base + 1)
                    scalar.activation(xc_full, xc_full, AF.Exp,
                                      scale=self.ppc("c_m1"))
                    scalar.drain()
                    scalar.activation(xc_full, xc_full, AF.Ln, bias=1.0)
                    scalar.drain().then_inc(sact.h, 1)
                v_act = sact.bump()
                self.sec(vector=f_dtv, scalar=f_dts)

                def f_bc(scalar, spj_base=spj_base):
                    scalar.wait_ge(spj.h, spj_base + 2 * KI + 1)
                    scalar.copy(bc_st[0][:], bcp_ps[:]).then_inc(sbc.h, 1)
                v_bc = sbc.bump()
                self.sec(scalar=f_bc)

                cw3 = self.ppr("convw").rearrange("p (k j) -> p k j", k=KI)
                ct3 = ctmp_b[:].rearrange("p (k t) -> p k t", k=KI)
                tm3 = xsb[0][:].bitcast(BF16).rearrange("p (k t) -> p k t",
                                                        k=KI)

                def f_cv(vector, c=c, sevx_base=sevx_base, scv_base=scv.v,
                         sv=None):
                    sv = scv_base
                    vector.wait_ge(sevx.h, sevx_base + KI)
                    vector.wait_ge(stl.h, stl_base0 + c + 1)
                    vector.tensor_mul(
                        ct3, xiv[:, :, 0:TC],
                        cw3[:, :, 0:1].broadcast_to((128, KI, TC))
                    ).then_inc(scv.h, 1)
                    sv += 1
                    for j in range(1, 4):
                        vector.tensor_mul(
                            tm3, xiv[:, :, j:j + TC],
                            cw3[:, :, j:j + 1].broadcast_to((128, KI, TC))
                        ).then_inc(scv.h, 1)
                        sv += 1
                        vector.wait_ge(scv.h, sv)
                        vector.tensor_add(ct3, ct3, tm3).then_inc(scv.h, 1)
                        sv += 1
                        vector.wait_ge(scv.h, sv)
                    vector.tensor_add(
                        ct3, ct3,
                        self.ppr("convb").unsqueeze(2).broadcast_to(
                            (128, KI, TC))).then_inc(scv.h, 1)

                def f_fin(vector, v_act=v_act, svd_base=svdt.v - 2):
                    vector.wait_ge(sact.h, v_act)
                    vector.tensor_add(dt_full, dt_full,
                                      xc_full).then_inc(svdt.h, 1)
                svdt.bump(3)
                scv.bump(8)
                self.sec(vector=f_cv)
                self.sec(vector=f_fin)

                def f_cs(scalar, scv_t=scv.v, svd_t=svdt.v):
                    scalar.wait_ge(scv.h, scv_t)
                    scalar.wait_ge(svdt.h, svd_t)
                    scalar.activation(xc_full, ct_full,
                                      AF.Silu).then_inc(sxc.h, 1)
                sxc.bump(1)
                self.sec(scalar=f_cs)

                def f_st(sync, c=c, t0=t0, v_bc=v_bc,
                         sdt_t=svdt.v, sxc_t=sxc.v):
                    sync.wait_ge(svdt.h, sdt_t)
                    sync.wait_ge(sxc.h, sxc_t)
                    sync.wait_ge(sbc.h, v_bc)
                    sync.dma_start(out=sdtv[:, :, t0:t0 + TC],
                                   in_=dt_st[0][:].rearrange(
                                       "p (k t) -> p k t", k=KI)).then_inc(sst[0].h, 16)
                    sync.dma_start(out=sxcv[:, :, t0:t0 + TC],
                                   in_=xc_st[0][:].rearrange(
                                       "p (k t) -> p k t", k=KI)).then_inc(sst[0].h, 16)
                    sync.dma_start(out=self.s_bc[:, t0:t0 + TC],
                                   in_=bc_st[0][:]).then_inc(sst[0].h, 16)
                sst[0].bump(48)
                self.sec(sync=f_st)

            def f_wait(sync):
                sync.wait_ge(sst[0].h, sst[0].v)
            self.sec(sync=f_wait)

    def mamba2(self, src, dst):
        self.begin_phase()
        NCH = self.NCH
        with contextlib.ExitStack() as pst:
            sbp = lambda n, s, d=F32: self.sb("m2" + n, s, d, pst)
            dtb = sbp("dt", [128, KI * TC], BF16)
            xcb = sbp("xc", [128, KI * TC], BF16)
            bcbB = sbp("bcB", [NS, TC], BF16)
            bcbC = sbp("bcC", [NS, TC], BF16)
            dxb = sbp("dx", [128, KI * TC], BF16)
            Bb = sbp("Bb", [128, NS * TC], BF16)
            Cb = sbp("Cb", [128, NS * TC], BF16)
            hb = [sbp(f"hb{i}", [128, NS * TC], BF16) for i in range(2)]
            sel16_sb = sbp("sel", [NS, NS * 128], BF16)
            yred = [sbp(f"yr{i}", [128, TC]) for i in range(2)]
            a_big = sbp("ab", [128, NS * TC])
            ub2 = sbp("ub", [128, NS * TC], BF16)
            tmpc = sbp("tc", [128, 16])
            opT_sb = sbp("op", [128, KI * D], BF16)

            bb_ps = self.ps("m2bb", [128, 4, TC], pst)
            op_ps = self.ps("m2oq", [128, 2, TC], pst)

            ldc = self.gsem("hw")
            wl = self.gsem("sw")
            sbb = self.gsem("c")
            sbe = self.gsem("c")
            sa = self.gsem("c")
            ssc = self.gsem("c")
            spl = self.gsem("c")
            syr = self.gsem("c")
            sop = self.gsem("c")
            sx1 = self.gsem("c")
            sst = [self.gsem("hw")]
            schk = self.gsem("c")
            sdx = self.gsem("c")
            svp = self.gsem("c")
            sstc = self.gsem("c")
            sxr = self.gsem("hw")
            schk_base0 = schk.v
            sst_base0 = sst[0].v
            ssc_g0 = ssc.v
            sa_g0 = sa.v

            def f_w(gpsimd):
                gpsimd.wait_ge(self.ph.h, self.ph.v)
                gpsimd.dma_start(
                    out=opT_sb[:].rearrange("p (kt m) -> p kt m", kt=KI),
                    in_=self.opT.rearrange("(kt p) m -> p kt m", p=128),
                ).then_inc(wl.h, 16)
                gpsimd.dma_start(out=sel16_sb[:], in_=self.sel16).then_inc(wl.h, 16)
            wl.bump(32)
            self.sec(gpsimd=f_w)

            srcv = src.rearrange("(kt p) t -> p kt t", p=128)
            dstv = dst.rearrange("(kt p) t -> p kt t", p=128)
            sdtv = self.s_dt.rearrange("(kt p) t -> p kt t", p=128)
            sxcv = self.s_xc.rearrange("(kt p) t -> p kt t", p=128)

            ab3 = a_big[:].rearrange("p (s t) -> p s t", s=NS)
            ubv = ub2[:].rearrange("p (u s t) -> p u s t", u=2, s=8)

            for c in range(NCH):
                t0 = c * TC

                def f_ld(sync, c=c, t0=t0):
                    if c >= 1:
                        sync.wait_ge(schk.h, schk_base0 + c)
                    sync.dma_start(out=dtb[:].rearrange("p (k t) -> p k t", k=KI),
                                   in_=sdtv[:, :, t0:t0 + TC]).then_inc(ldc.h, 16)
                    sync.dma_start(out=xcb[:].rearrange("p (k t) -> p k t", k=KI),
                                   in_=sxcv[:, :, t0:t0 + TC]).then_inc(ldc.h, 16)
                    sync.dma_start(out=bcbB[:],
                                   in_=self.s_bc[0:NS, t0:t0 + TC]).then_inc(ldc.h, 16)
                    sync.dma_start(out=bcbC[:],
                                   in_=self.s_bc[NS:2 * NS, t0:t0 + TC]).then_inc(ldc.h, 16)
                v_ld = ldc.bump(64)
                self.sec(sync=f_ld)
                xres = dxb[:].bitcast(F32).rearrange("p (k t) -> p k t", k=KD)

                sbb_base = sbb.v
                sbe_base = sbe.v

                def f_bb(tensor, v_ld=v_ld, sbb_base=sbb_base, sbe_base=sbe_base, c=c):
                    tensor.wait_ge(ldc.h, v_ld)
                    if c == 0:
                        tensor.wait_ge(wl.h, wl.v)
                    for grp in range(8):
                        if c > 0 or grp >= 1:
                            tensor.wait_ge(sbe.h, sbe_base + grp)
                        mvg = bcbB if grp < 4 else bcbC
                        for j in range(4):
                            s = (grp % 4) * 4 + j
                            mm = tensor.matmul(
                                bb_ps[:, j, :],
                                sel16_sb[:, s * 128:(s + 1) * 128],
                                mvg[:], start=True, stop=True)
                        mm.then_inc(sbb.h, 1)
                sbb.bump(8)

                def f_be(scalar, sbb_base=sbb_base, c=c):
                    for grp in range(8):
                        scalar.wait_ge(sbb.h, sbb_base + grp + 1)
                        dst_b = (Bb if grp < 4 else Cb)
                        g4 = (grp % 4) * 4
                        scalar.copy(
                            dst_b[:].rearrange("p (s t) -> p s t", s=NS)[:, g4:g4 + 4, :],
                            bb_ps[:]).then_inc(sbe.h, 1)
                sbe.bump(8)
                self.sec(tensor=f_bb, scalar=f_be)

                def f_dx(vector, v_ld=v_ld):
                    vector.wait_ge(ldc.h, v_ld)
                    vector.tensor_mul(dxb[:], dtb[:], xcb[:]).then_inc(sdx.h, 1)
                v_dx = sdx.bump()
                self.sec(vector=f_dx)

                dtb3 = dtb[:].rearrange("p (k t) -> p k t", k=KI)
                dxb3 = dxb[:].rearrange("p (k t) -> p k t", k=KI)
                xcb3 = xcb[:].rearrange("p (k t) -> p k t", k=KI)
                ybf3 = xcb3

                sa_base = sa.v
                ssc_base = ssc.v
                spl_base = spl.v
                syr_base = syr.v
                sstc_base = sstc.v

                def f_a(scalar, v_ld=v_ld, c=c):
                    scalar.wait_ge(ldc.h, v_ld)
                    for st in range(2 * KI):
                        dhi, h = divmod(st, 2)
                        g = 32 * c + st
                        if g >= 2:
                            scalar.wait_ge(ssc.h, ssc_g0 + g - 1)
                        for j in range(8):
                            s = h * 8 + j
                            mm = scalar.activation(
                                ab3[:, s, :], dtb3[:, dhi, :], AF.Exp,
                                scale=self.ppc("A", dhi * NS + s))
                        mm.then_inc(sa.h, 1)
                sa.bump(2 * KI)

                def _stt(vector, dhi):
                    vector.scalar_tensor_tensor(
                        ybf3[:, dhi, :], xcb3[:, dhi, :],
                        self.ppc("Dm", dhi), yred[dhi % 2][:],
                        ALU.mult, ALU.add).then_inc(syr.h, 1)

                def f_scan(vector, c=c, sa_base=sa_base, ssc_base=ssc_base,
                           spl_base=spl_base, v_dx=v_dx, svp_base=svp.v,
                           sbe_base=sbe_base, sstc_base=sstc_base):
                    vector.wait_ge(sdx.h, v_dx)
                    vector.wait_ge(sbe.h, sbe_base + 8)
                    vector.wait_ge(sst[0].h, sst_base0 + 16 * c)
                    svp_v = svp_base
                    for dhi in range(KI):
                        par = dhi % 2
                        hb3 = hb[par][:].rearrange("p (s t) -> p s t", s=NS)
                        for h in range(2):
                            st = 2 * dhi + h
                            g = 32 * c + st
                            up = st % 2
                            s0 = h * 8
                            if g >= 2:
                                vector.wait_ge(ssc.h, ssc_g0 + g - 1)
                            vector.tensor_mul(
                                ubv[:, up, :, :],
                                dxb3[:, dhi:dhi + 1, :].broadcast_to(
                                    (128, 8, TC)),
                                Bb[:].rearrange("p (s t) -> p s t", s=NS)
                                [:, s0:s0 + 8, :]).then_inc(svp.h, 1)
                            svp_v += 1
                            vector.wait_ge(sa.h, sa_g0 + g + 1)
                            if c >= 1:
                                vector.wait_ge(sstc.h,
                                               sstc_base - KI + dhi + 1)
                            vector.tensor_mul(
                                tmpc[:, s0:s0 + 8].unsqueeze(2),
                                ab3[:, s0:s0 + 8, 0:1],
                                self.mst[:, dhi * NS + s0:dhi * NS + s0 + 8]
                                .unsqueeze(2)).then_inc(svp.h, 1)
                            svp_v += 1
                            vector.wait_ge(svp.h, svp_v)
                            vector.tensor_add(
                                ubv[:, up, :, 0:1],
                                ubv[:, up, :, 0:1],
                                tmpc[:, s0:s0 + 8].unsqueeze(2)
                            ).then_inc(svp.h, 1)
                            svp_v += 1
                            vector.memset(ab3[:, s0:s0 + 8, 0:1],
                                          0.0).then_inc(svp.h, 1)
                            svp_v += 1
                            vector.wait_ge(svp.h, svp_v)
                            if h == 0 and dhi >= 2:
                                vector.wait_ge(spl.h, spl_base + dhi - 1)
                            vector.tensor_tensor_scan(
                                hb[par][:, s0 * TC:(s0 + 8) * TC],
                                a_big[:, s0 * TC:(s0 + 8) * TC],
                                ub2[:, up * 8 * TC:(up + 1) * 8 * TC],
                                0.0, ALU.mult, ALU.add).then_inc(ssc.h, 1)
                        vector.wait_ge(ssc.h, ssc_g0 + 32 * c + 2 * dhi + 2)
                        vector.tensor_copy(self.mst[:, dhi * NS:(dhi + 1) * NS],
                                           hb3[:, :, TC - 1]).then_inc(sstc.h, 1)

                        def _tree(pd):
                            nonlocal svp_v
                            H = hb[pd % 2]
                            vector.wait_ge(spl.h, spl_base + pd + 1)
                            for w in (8, 4, 2):
                                vector.tensor_add(
                                    H[:, 0:w * TC], H[:, 0:w * TC],
                                    H[:, w * TC:2 * w * TC]).then_inc(svp.h, 1)
                                svp_v += 1
                                vector.wait_ge(svp.h, svp_v)
                            vector.tensor_add(yred[pd % 2][:], H[:, 0:TC],
                                              H[:, TC:2 * TC]).then_inc(svp.h, 1)
                            svp_v += 1

                        if dhi >= 1:
                            _tree(dhi - 1)
                        if dhi >= 2:
                            vector.wait_ge(svp.h, svp_v)
                            _stt(vector, dhi - 2)
                    _tree(KI - 1)
                    vector.wait_ge(svp.h, svp_v)
                    _stt(vector, KI - 2)
                    _stt(vector, KI - 1)
                ssc.bump(2 * KI)
                svp.bump(12 * KI)
                syr.bump(KI)
                sstc.bump(KI)

                def f_pl(gpsimd, ssc_base=ssc_base, sstc_base=sstc_base,
                         sbe_base=sbe_base, syr_base=syr_base):
                    gpsimd.wait_ge(sbe.h, sbe_base + 8)
                    for dhi in range(KI):
                        gpsimd.wait_ge(ssc.h, ssc_base + (dhi + 1) * 2)
                        gpsimd.wait_ge(sstc.h, sstc_base + dhi + 1)
                        if dhi >= 2:
                            gpsimd.wait_ge(syr.h, syr_base + dhi - 1)
                        gpsimd.tensor_mul(hb[dhi % 2][:], hb[dhi % 2][:],
                                          Cb[:]).then_inc(spl.h, 1)
                spl.bump(KI)
                self.sec(scalar=f_a, vector=f_scan, gpsimd=f_pl)

                def f_ldx(sync, c=c, t0=t0, ssc_t=ssc.v):
                    sync.wait_ge(ssc.h, ssc_t)
                    sync.dma_start(out=xres, in_=srcv[:, :, t0:t0 + TC]
                                   ).then_inc(sxr.h, 16)
                sxr.bump(16)
                self.sec(sync=f_ldx)

                sop_base = sop.v
                sx1_base = sx1.v

                def f_op(tensor, syr_t=syr.v, sop_base=sop_base,
                         sx1_base=sx1_base, c=c):
                    tensor.wait_ge(syr.h, syr_t)
                    for mt in range(KD):
                        bank = mt % 2
                        if mt >= 2:
                            tensor.wait_ge(sx1.h, sx1_base + mt - 1)
                        for kt in range(KI):
                            mm = tensor.matmul(
                                op_ps[:, bank, :],
                                opT_sb[:].rearrange("p (k m) -> p k m", k=KI)
                                [:, kt, mt * 128:(mt + 1) * 128],
                                ybf3[:, kt, :],
                                start=(kt == 0), stop=(kt == KI - 1))
                        mm.then_inc(sop.h, 1)
                sop.bump(KD)

                def f_x1(vector, sop_base=sop_base, c=c, v_xr=sxr.v,
                         sxr=sxr, xres=xres):
                    x1v = ub2[:].bitcast(F32).rearrange("p (k t) -> p k t", k=KD)
                    if c >= 1:
                        vector.wait_ge(sst[0].h, sst_base0 + 16 * c)
                    vector.wait_ge(sxr.h, v_xr)
                    for mt in range(KD):
                        vector.wait_ge(sop.h, sop_base + mt + 1)
                        vector.tensor_add(x1v[:, mt, :], xres[:, mt, :],
                                          op_ps[:, mt % 2, :]).then_inc(sx1.h, 1)
                    vector.drain().then_inc(schk.h, 1)
                sx1.bump(KD)
                schk.bump()
                self.sec(tensor=f_op, vector=f_x1)

                def f_st(sync, t0=t0, sx1_t=sx1.v):
                    sync.wait_ge(sx1.h, sx1_t)
                    sync.dma_start(out=dstv[:, :, t0:t0 + TC],
                                   in_=ub2[:].bitcast(F32).rearrange(
                                       "p (k t) -> p k t", k=KD)).then_inc(sst[0].h, 16)
                sst[0].bump(16)
                self.sec(sync=f_st)

            def f_wait(sync):
                sync.wait_ge(sst[0].h, sst[0].v)
            self.sec(sync=f_wait)

    def ffn_pre(self, src, call):
        self.begin_phase()
        NCH = self.NCH
        tg = f"f{call}p"
        with contextlib.ExitStack() as pst:
            sbp = lambda n, s, d=F32: self.sb(tg + n, s, d, pst)
            xsb = sbp("x", [128, KD * TC])
            lnb = self.mk_ln_bufs(pst, tg + "l", TC)
            u0 = sbp("u0", [128, KD * TC])
            n_f = sbp("nf", [128, KD * TC])
            n_b = sbp("nb", [128, KD * TC], BF16)
            x2 = sbp("x2", [128, KD * TC])
            n2b = sbp("n2", [128, KD * TC], BF16)
            yin = sbp("yi", [128, KD * TC])
            mixw_sb = sbp("mw", [128, KD * D], BF16)
            rw_sb = sbp("rw", [128, KD * NE], BF16)
            eye_sb = sbp("ey", [128, 128])
            b2s = sbp("b2", [NE, D], BF16)
            lg = sbp("lg", [8, TC])
            prb = sbp("pr", [128, NE])
            wo_ = sbp("wo", [128, NE])
            r1 = sbp("r1", [128, 1])
            r2 = sbp("r2", [128, 1])
            cmb = sbp("cm", [128, NE])
            cbT = sbp("cb", [NE, TC], BF16)
            mt_t = sbp("mt", [128, TC])

            mix_ps = self.ps(tg + "mp", [128, 2, TC], pst)
            rt_ps = self.ps(tg + "rp", [8, TC], pst)
            tt_ps = self.ps(tg + "tt", [128, NE + 128], pst)

            wl = self.gsem("sw")
            xl = self.gsem("hw")
            smix = self.gsem("c")
            sme = self.gsem("c")
            smb = self.gsem("c")
            srt = self.gsem("c")
            str_ = self.gsem("c")
            ssm = self.gsem("c")
            sse = self.gsem("c")
            scb = self.gsem("c")
            sb2 = self.gsem("c")
            syi = self.gsem("c")
            sx2 = self.gsem("c")
            sst = self.gsem("hw")

            def f_w(gpsimd):
                gpsimd.wait_ge(self.ph.h, self.ph.v)
                gpsimd.dma_start(
                    out=mixw_sb[:].rearrange("p (kt m) -> p kt m", kt=KD),
                    in_=self.mixw.rearrange("(kt p) m -> p kt m", p=128),
                ).then_inc(wl.h, 16)
                gpsimd.dma_start(
                    out=rw_sb[:].rearrange("p (kt m) -> p kt m", kt=KD),
                    in_=self.routerw.rearrange("(kt p) m -> p kt m", p=128),
                ).then_inc(wl.h, 16)
                gpsimd.dma_start(out=b2s[:], in_=self.b2).then_inc(wl.h, 16)
            wl.bump(48)

            wlh = self.gsem("hw")

            def f_w2(sync):
                sync.wait_ge(self.ph.h, self.ph.v)
                sync.dma_start(out=eye_sb[:], in_=self.eye32).then_inc(wlh.h, 16)
            wlh.bump(16)
            self.sec(sync=f_w2, gpsimd=f_w)

            srcv = src.rearrange("(kt p) t -> p kt t", p=128)
            xwv = self.x_work.rearrange("(kt p) t -> p kt t", p=128)
            n2v = self.s_n2.rearrange("(kt p) t -> p kt t", p=128)
            yav = self.s_ya.rearrange("(kt p) t -> p kt t", p=128)

            u0v = u0[:].rearrange("p (k t) -> p k t", k=KD)
            nfv = n_f[:].rearrange("p (k t) -> p k t", k=KD)
            nbv = n_b[:].rearrange("p (k t) -> p k t", k=KD)
            x2v = x2[:].rearrange("p (k t) -> p k t", k=KD)
            n2bv = n2b[:].rearrange("p (k t) -> p k t", k=KD)
            yiv = yin[:].rearrange("p (k t) -> p k t", k=KD)
            x3 = xsb[:].rearrange("p (k t) -> p k t", k=KD)

            done_prev = [0]
            for c in range(NCH):
                t0 = c * TC

                def f_ld(sync, c=c, t0=t0, dp=done_prev[0]):
                    if c >= 1:
                        sync.wait_ge(sst.h, dp)
                    sync.dma_start(out=x3, in_=srcv[:, :, t0:t0 + TC]
                                   ).then_inc(xl.h, 16)
                v_x = xl.bump(16)
                self.sec(sync=f_ld)

                def f_gate(eng, v=v_x, c=c, smix_t=smix.v, sst_t=sst.v):
                    eng.wait_ge(xl.h, v)
                    eng.wait_ge(wl.h, wl.v)
                    eng.wait_ge(wlh.h, wlh.v)
                    if c >= 1:
                        eng.wait_ge(smix.h, smix_t)
                        eng.wait_ge(sst.h, sst_t)
                self.sec(tensor=f_gate, vector=f_gate, scalar=f_gate)

                v_u0 = self.emit_ln(lnb, x3, TC, out_f32=u0v)
                v_n = self.emit_ln(lnb, u0v, TC, out_f32=nfv, out_b16=nbv,
                                   g=self.ppr("g1"), b=self.ppr("be1"),
                                   wait_sv=v_u0)

                smix_base = smix.v
                sme_base = sme.v

                def f_mm(tensor, v_n=v_n, smix_base=smix_base,
                         sme_base=sme_base, c=c, syi_t=syi.v):
                    tensor.wait_ge(lnb["sv"].h, v_n)
                    if c >= 1:
                        tensor.wait_ge(syi.h, syi_t)
                    wv = mixw_sb[:].rearrange("p (kt m) -> p kt m", kt=KD)
                    for mt in range(KD):
                        if mt >= 2:
                            tensor.wait_ge(sme.h, sme_base + mt - 1)
                        for kt in range(KD):
                            mm = tensor.matmul(
                                mix_ps[:, mt % 2, :],
                                wv[:, kt, mt * 128:(mt + 1) * 128],
                                nbv[:, kt, :], start=(kt == 0),
                                stop=(kt == KD - 1))
                        mm.then_inc(smix.h, 1)
                smix.bump(KD)

                def f_me(vector, smix_base=smix_base, sme_base=sme.v,
                         smb_base=smb.v, c=c):
                    for mt in range(KD):
                        vector.wait_ge(smix.h, smix_base + mt + 1)
                        vector.tensor_scalar(x2v[:, mt, :], mix_ps[:, mt % 2, :],
                                             self.ppc("mixb", mt), None,
                                             ALU.add).then_inc(sme.h, 1)
                    vector.wait_ge(sme.h, sme_base + KD)
                    vector.tensor_add(x2[:], x2[:], u0[:]).then_inc(smb.h, 1)
                    vector.wait_ge(smb.h, smb_base + 1)
                    vector.tensor_add(x2[:], x2[:], n_f[:]).then_inc(smb.h, 1)
                sme.bump(KD)
                smb.bump(2)
                self.sec(tensor=f_mm, vector=f_me)

                def f_g2gate(eng, sme_t=sme.v, smb_t=smb.v):
                    eng.wait_ge(sme.h, sme_t)
                    eng.wait_ge(smb.h, smb_t)
                self.sec(vector=f_g2gate, scalar=f_g2gate, tensor=f_g2gate)
                v_n2 = self.emit_ln(lnb, x2v, TC, out_b16=n2bv,
                                    g=self.ppr("g2"), b=self.ppr("be2"))

                def f_rt(tensor, v_n2=v_n2):
                    tensor.wait_ge(lnb["sv"].h, v_n2)
                    rv = rw_sb[:].rearrange("p (kt m) -> p kt m", kt=KD)
                    for kt in range(KD):
                        mm = tensor.matmul(rt_ps[:], rv[:, kt, :],
                                           n2bv[:, kt, :],
                                           start=(kt == 0), stop=(kt == KD - 1))
                    mm.then_inc(srt.h, 1)
                v_rt = srt.bump()
                self.sec(tensor=f_rt)

                str_base = str_.v
                ssm_base = ssm.v
                sse_base = sse.v
                scb_base = scb.v

                def f_rte(scalar, v_rt=v_rt):
                    scalar.wait_ge(srt.h, v_rt)
                    scalar.copy(lg[:], rt_ps[:])
                    scalar.drain().then_inc(sse.h, 1)
                v_lg = sse.bump()

                def f_tr(tensor, v_lg=v_lg, str_base=str_base,
                         scb_base=scb_base, ssm_base=ssm_base):
                    tensor.wait_ge(sse.h, v_lg)
                    for q in range(TC // 128):
                        if q >= 1:
                            tensor.wait_ge(scb.h, scb_base + q)
                        mm = tensor.transpose(
                            tt_ps[:, 0:NE], lg[:, q * 128:(q + 1) * 128],
                            eye_sb[0:NE, 0:NE])
                        mm.then_inc(str_.h, 1)
                        tensor.wait_ge(ssm.h, ssm_base + 2 * q + 2)
                        mm = tensor.transpose(tt_ps[0:NE, NE:NE + 128], cmb[:],
                                              eye_sb[:])
                        mm.then_inc(str_.h, 1)
                str_.bump(2 * (TC // 128))

                def f_sm(vector, str_base=str_base, sse_base=sse_base,
                         scb_base=scb_base, ssm_base=ssm_base):
                    for q in range(TC // 128):
                        vector.wait_ge(str_.h, str_base + 2 * q + 1)
                        vector.tensor_copy(prb[:], tt_ps[:, 0:NE])
                        vector.drain()
                        vector.reduce_max(r1[:], prb[:], axis=AX.X, negate=True)
                        vector.drain().then_inc(ssm.h, 1)
                        vector.wait_ge(sse.h, sse_base + 1 + q + 1)
                        vector.reduce_sum(r2[:], prb[:], axis=AX.X)
                        vector.drain()
                        vector.reciprocal(r2[:], r2[:])
                        vector.drain()
                        vector.tensor_scalar(prb[:], prb[:], r2[:], None,
                                             ALU.mult)
                        vector.drain()
                        vector.reduce_max(r1[:], prb[:], axis=AX.X)
                        vector.drain()
                        vector.tensor_scalar(wo_[:], prb[:], r1[:], None,
                                             ALU.is_ge)
                        vector.drain()
                        vector.scalar_tensor_tensor(wo_[:], wo_[:], -1e30,
                                                    prb[:], ALU.mult, ALU.add)
                        vector.drain()
                        vector.reduce_max(r2[:], wo_[:], axis=AX.X)
                        vector.drain()
                        vector.tensor_scalar(wo_[:], prb[:], r2[:], None,
                                             ALU.is_ge)
                        vector.drain()
                        if q >= 1:
                            vector.wait_ge(str_.h, str_base + 2 * q)
                        vector.tensor_mul(cmb[:], prb[:], wo_[:])
                        vector.drain().then_inc(ssm.h, 1)
                ssm.bump(2 * (TC // 128))

                def f_se(scalar, ssm_base=ssm_base, sse_base=sse_base,
                         str_base=str_base):
                    for q in range(TC // 128):
                        scalar.wait_ge(ssm.h, ssm_base + 2 * q + 1)
                        scalar.activation(prb[:], prb[:], AF.Exp, bias=r1[:])
                        scalar.drain().then_inc(sse.h, 1)
                        scalar.wait_ge(str_.h, str_base + 2 * q + 2)
                        scalar.copy(cbT[:, q * 128:(q + 1) * 128],
                                    tt_ps[:NE, NE:NE + 128])
                        scalar.drain().then_inc(scb.h, 1)
                sse.bump(TC // 128)
                scb.bump(TC // 128)
                self.sec(scalar=f_rte)
                self.sec(tensor=f_tr, vector=f_sm, scalar=f_se)

                sb2_base = sb2.v
                syi_base = syi.v

                def f_b2(tensor, scb_t=scb.v, sb2_base=sb2_base,
                         syi_base=syi_base, sme_t=sme.v):
                    tensor.wait_ge(scb.h, scb_t)
                    tensor.wait_ge(sme.h, sme_t)
                    for mt in range(KD):
                        if mt >= 2:
                            tensor.wait_ge(syi.h, syi_base + mt - 1)
                        mm = tensor.matmul(mix_ps[:, mt % 2, :],
                                           b2s[:, mt * 128:(mt + 1) * 128],
                                           cbT[:], start=True, stop=True)
                        mm.then_inc(sb2.h, 1)
                sb2.bump(KD)

                def f_yi(vector, sb2_base=sb2_base, syi_base=syi.v,
                         smb_base=smb.v):
                    for mt in range(KD):
                        vector.wait_ge(sb2.h, sb2_base + mt + 1)
                        vector.tensor_add(yiv[:, mt, :], x3[:, mt, :],
                                          mix_ps[:, mt % 2, :]).then_inc(syi.h, 1)
                    vector.wait_ge(syi.h, syi_base + KD)
                    vector.tensor_add(yin[:], yin[:], x2[:]).then_inc(smb.h, 1)
                syi.bump(KD)
                smb.bump(1)
                self.sec(tensor=f_b2, vector=f_yi)

                def f_st(sync, c=c, t0=t0, v_n2=v_n2, syi_t=syi.v,
                         scb_t=scb.v, smb_t=smb.v):
                    sync.wait_ge(lnb["sv"].h, v_n2)
                    sync.wait_ge(syi.h, syi_t)
                    sync.wait_ge(smb.h, smb_t)
                    sync.wait_ge(scb.h, scb_t)
                    sync.dma_start(out=xwv[:, :, t0:t0 + TC], in_=x2v
                                   ).then_inc(sst.h, 16)
                    sync.dma_start(out=n2v[:, :, t0:t0 + TC], in_=n2bv
                                   ).then_inc(sst.h, 16)
                    sync.dma_start(out=yav[:, :, t0:t0 + TC], in_=yiv
                                   ).then_inc(sst.h, 16)
                    sync.dma_start(out=self.s_cb[:, t0:t0 + TC], in_=cbT[:]
                                   ).then_inc(sst.h, 16)
                done_prev[0] = sst.bump(64)
                self.sec(sync=f_st)

            def f_wait(sync):
                sync.wait_ge(sst.h, sst.v)
            self.sec(sync=f_wait)

    def ffn_exp(self, dst, call):
        self.begin_phase()
        NCH = self.NCH
        tg = f"f{call}e"
        with contextlib.ExitStack() as pst:
            sbp = lambda n, s, d=F32: self.sb(tg + n, s, d, pst)
            w1sb = sbp("w1", [128, KD * FF], BF16)
            w2sb = sbp("w2", [128, KF * D], BF16)
            n2s = [sbp(f"n2{i}", [128, KD * TC], BF16) for i in range(2)]
            h_sb = sbp("h", [128, KF * TC], BF16)
            cmbs = [sbp(f"cT{i}", [NE, TC], BF16) for i in range(2)]
            sel8s = sbp("s8", [NE, NE * 128], BF16)
            cbb = [sbp(f"cb{i}", [128, TC]) for i in range(2)]
            yab = sbp("ya", [128, KD * TC])
            tmp = sbp("tp", [128, TC])

            h_ps = self.ps(tg + "hp", [128, 2, TC], pst)
            y_ps = self.ps(tg + "yp", [128, 4, TC], pst)
            cb_ps = self.ps(tg + "cp", [128, TC], pst)

            wl1p = [self.gsem("sw") for _ in range(4)]
            wl2p = [self.gsem("sw") for _ in range(2)]
            cl = self.gsem("sw")
            nl = [self.gsem("hw") for i in range(2)]
            yl = self.gsem("hw")
            shp = self.gsem("c")
            she = self.gsem("c")
            syp = self.gsem("c")
            sfc = self.gsem("c")
            sye = self.gsem("c")
            smu = self.gsem("c")
            scb = self.gsem("c")
            sce = self.gsem("c")
            sstY = self.gsem("hw")
            sA = self.gsem("c")
            wl1_b = [s.v for s in wl1p]
            wl2_b = [s.v for s in wl2p]
            shp_b = shp.v
            she_b = she.v
            syp_b = syp.v
            sye_b = sye.v
            smu_b = smu.v
            sce_b = sce.v

            n2v = self.s_n2.rearrange("(kt p) t -> p kt t", p=128)
            yav = self.s_ya.rearrange("(kt p) t -> p kt t", p=128)
            dstv = dst.rearrange("(kt p) t -> p kt t", p=128)
            w1v = w1sb[:].rearrange("p (kt m) -> p kt m", kt=KD)
            w2v = w2sb[:].rearrange("p (kt m) -> p kt m", kt=KF)
            h3 = h_sb[:].rearrange("p (kt t) -> p kt t", kt=KF)
            yabv = yab[:].rearrange("p (k t) -> p k t", k=KD)

            def f_cl(gpsimd):
                gpsimd.wait_ge(self.ph.h, self.ph.v)
                gpsimd.dma_start(out=sel8s[:], in_=self.sel8).then_inc(cl.h, 16)
            cl.bump(16)
            self.sec(gpsimd=f_cl)

            def f_ws(gpsimd):
                gpsimd.wait_ge(self.ph.h, self.ph.v)
                for e in range(NE):
                    for part in range(4):
                        if e >= 1:
                            gpsimd.wait_ge(
                                shp.h, shp_b + KF * (e * NCH - 1)
                                + 8 * (part + 1))
                        gpsimd.dma_start(
                            out=w1v[:, :, part * (FF // 4):(part + 1) * (FF // 4)],
                            in_=self.w1.rearrange("e (kt p) m -> e p kt m", p=128)
                            [e, :, :, part * (FF // 4):(part + 1) * (FF // 4)],
                        ).then_inc(wl1p[part].h, 16)
                    for part in range(2):
                        if e >= 1:
                            gpsimd.wait_ge(
                                syp.h, syp_b + 8 * (e * NCH - 1)
                                + 4 * (part + 1))
                        gpsimd.dma_start(
                            out=w2sb[:].rearrange("p (kt m) -> p kt m", kt=KF)
                            [:, :, part * (D // 2):(part + 1) * (D // 2)],
                            in_=self.w2.rearrange("e (kt p) m -> e p kt m", p=128)
                            [e, :, :, part * (D // 2):(part + 1) * (D // 2)],
                        ).then_inc(wl2p[part].h, 16)
            for s_ in wl1p:
                s_.bump(16 * NE)
            for s_ in wl2p:
                s_.bump(16 * NE)
            self.sec(gpsimd=f_ws)

            for e in range(NE):
                for sc in range(NCH):
                    t0 = sc * TC
                    sl = (e * NCH + sc) % 2
                    cb_sl = (e * NCH + sc) % 2
                    esc = e * NCH + sc

                    def f_n2(scalar, e=e, sc=sc, t0=t0, sl=sl, esc=esc,
                             scb_t=scb.v):
                        if esc >= 2:
                            scalar.wait_ge(shp.h, shp_b + KF * (esc - 1))
                            scalar.wait_ge(scb.h, scb_t - 1)
                        scalar.dma_start(out=n2s[sl][:].rearrange(
                            "p (k t) -> p k t", k=KD),
                            in_=n2v[:, :, t0:t0 + TC]).then_inc(nl[sl].h, 16)
                        scalar.dma_start(out=cmbs[sl][:],
                                       in_=self.s_cb[:, t0:t0 + TC]
                                       ).then_inc(nl[sl].h, 16)
                    v_n2 = nl[sl].bump(32)

                    def f_ya(sync, t0=t0, esc=esc, sstY_t=sstY.v):
                        if esc >= 1:
                            sync.wait_ge(sstY.h, sstY_t)
                        sync.dma_start(out=yabv, in_=yav[:, :, t0:t0 + TC]
                                       ).then_inc(yl.h, 16)
                    v_ya = yl.bump(16)
                    self.sec(scalar=f_n2)

                    def f_cb(tensor, e=e, t0=t0, sl=sl, esc=esc, v_cl=cl.v,
                             sce_t=sce.v, v_n2=v_n2):
                        tensor.wait_ge(cl.h, v_cl)
                        if esc >= 1:
                            tensor.wait_ge(sce.h, sce_b + esc)
                        tensor.wait_ge(nl[sl].h, v_n2)
                        mm = tensor.matmul(
                            cb_ps[:], sel8s[:, e * 128:(e + 1) * 128],
                            cmbs[sl][:], start=True, stop=True)
                        mm.then_inc(scb.h, 1)
                    v_cb = scb.bump()

                    def f_cbe(vector, sl=cb_sl, v_cb=v_cb, esc=esc):
                        vector.wait_ge(scb.h, v_cb)
                        if esc >= 2:
                            vector.wait_ge(smu.h, smu_b + 8 * (esc - 1))
                        vector.tensor_copy(cbb[sl][:], cb_ps[:]).then_inc(sce.h, 1)
                    sce.bump()
                    self.sec(tensor=f_cb, vector=f_cbe)

                    shp_base = shp.v
                    she_base = she.v

                    def f_A(tensor, e=e, sl=sl, esc=esc, v_n2=v_n2,
                            shp_base=shp_base, she_base=she_base):
                        tensor.wait_ge(nl[sl].h, v_n2)
                        n2t = n2s[sl][:].rearrange("p (k t) -> p k t", k=KD)
                        if esc >= 1:
                            tensor.wait_ge(she.h, she_b + KF * esc)
                        for ft in range(KF):
                            if ft % 8 == 0:
                                p_ = ft // 8
                                tensor.wait_ge(wl1p[p_].h,
                                               wl1_b[p_] + 16 * (e + 1))
                            if ft >= 2:
                                tensor.wait_ge(she.h, she_base + ft - 1)
                            for kt in range(KD):
                                mm = tensor.matmul(
                                    h_ps[:, ft % 2, :],
                                    w1v[:, kt, ft * 128:(ft + 1) * 128],
                                    n2t[:, kt, :],
                                    start=(kt == 0), stop=(kt == KD - 1))
                            mm.then_inc(shp.h, 1)
                    shp.bump(KF)

                    def f_he(scalar, e=e, shp_base=shp_base, esc=esc):
                        if esc >= 1:
                            scalar.wait_ge(syp.h, syp_b + 8 * esc)
                        for ft in range(KF):
                            scalar.wait_ge(shp.h, shp_base + ft + 1)
                            scalar.activation(h3[:, ft, :], h_ps[:, ft % 2, :],
                                              AF.Silu,
                                              bias=self.ppc("b1", e * KF + ft)
                                              ).then_inc(she.h, 1)
                        scalar.drain().then_inc(sA.h, 1)
                    she.bump(KF)
                    sA.bump()
                    self.sec(tensor=f_A, scalar=f_he)
                    self.sec(sync=f_ya)

                    syp_base = syp.v
                    sye_base = sye.v
                    smu_base = smu.v

                    def f_B(tensor, e=e, esc=esc, she_b2=she.v - KF,
                            syp_base=syp_base, sye_base=sye_base):
                        if esc >= 1:
                            tensor.wait_ge(smu.h, smu_b + 8 * esc)
                        for half in range(2):
                            tensor.wait_ge(wl2p[half].h,
                                           wl2_b[half] + 16 * (e + 1))
                            for mi in range(4):
                                mt = half * 4 + mi
                                gy = half * 4 + mi
                                if gy >= 4:
                                    tensor.wait_ge(smu.h, smu_base + gy - 4 + 1)
                                for kt in range(KF):
                                    if gy == 0:
                                        tensor.wait_ge(she.h, she_b2 + kt + 1)
                                    mm = tensor.matmul(
                                        y_ps[:, mi, :],
                                        w2v[:, kt, mt * 128:(mt + 1) * 128],
                                        h3[:, kt, :],
                                        start=(kt == 0), stop=(kt == KF - 1))
                                mm.then_inc(syp.h, 1)
                    syp.bump(8)

                    def f_ye(vector, sl=cb_sl, esc=esc, v_ya=v_ya,
                             syp_base=syp_base, sce_t=sce.v):
                        vector.wait_ge(yl.h, v_ya)
                        vector.wait_ge(sce.h, sce_t)
                        for gy in range(8):
                            mi = gy % 4
                            mt = gy
                            vector.wait_ge(syp.h, syp_base + gy + 1)
                            if gy >= 1:
                                vector.drain()
                            vector.tensor_mul(tmp[:], y_ps[:, mi, :],
                                              cbb[sl][:]).then_inc(smu.h, 1)
                            vector.drain()
                            vector.tensor_add(yabv[:, mt, :], yabv[:, mt, :],
                                              tmp[:]).then_inc(sye.h, 1)
                    sye.bump(8)
                    smu.bump(8)
                    self.sec(tensor=f_B, vector=f_ye)

                    def f_st(sync, e=e, t0=t0, sye_t=sye.v):
                        sync.wait_ge(sye.h, sye_t)
                        out = dstv if e == NE - 1 else yav
                        sync.dma_start(out=out[:, :, t0:t0 + TC], in_=yabv
                                       ).then_inc(sstY.h, 16)
                    sstY.bump(16)
                    self.sec(sync=f_st)

            def f_wait(sync):
                sync.wait_ge(sstY.h, sstY.v)
            self.sec(sync=f_wait)

    def rwkv1(self, src):
        self.begin_phase()
        NCH = self.NCH
        TW = TC + 2
        tg = "r1"
        with contextlib.ExitStack() as pst:
            sbp = lambda n, s, d=F32: self.sb(tg + n, s, d, pst)
            xw = sbp("xw", [128, KD * TW])
            car = sbp("car", [128, KD])
            lnb = self.mk_ln_bufs(pst, tg + "l", TC)
            xm_b = sbp("xm", [128, KD * TC], BF16)
            wk_sb = sbp("wk", [128, KD * D], BF16)
            wv_sb = sbp("wv", [128, KD * D], BF16)
            wr_sb = sbp("wr", [128, KD * D], BF16)
            wo_sb = sbp("wo", [128, KD * D], BF16)
            k_sb = sbp("k", [128, KD * TC])
            v_sb = sbp("v", [128, KD * TC])
            sr_sb = sbp("sr", [128, KD * TC])
            zb = sbp("z", [128, KD * TC], BF16)
            kv_t = sbp("kv", [128, TC])
            ab_buf = sbp("ab", [128, TW - 1])
            bb_buf = sbp("bb", [128, TW - 1])
            t1 = sbp("t1", [128, TC])
            t2 = sbp("t2", [128, TC])
            tb1 = sbp("tb", [128, KD * TC], BF16)

            kvr_ps = self.ps(tg + "kp", [128, 2, TC], pst)
            ao_ps = self.ps(tg + "ap", [128, 2, TC], pst)

            wl = self.gsem("sw")
            xl = self.gsem("hw")
            spj = self.gsem("c")
            sev = self.gsem("c")
            swk = self.gsem("c")
            sao = self.gsem("c")
            su2 = self.gsem("c")
            sst = self.gsem("hw")
            smx = self.gsem("c")
            smv = self.gsem("c")
            sza = self.gsem("c")

            def f_w(gpsimd):
                gpsimd.wait_ge(self.ph.h, self.ph.v)
                for dst_sb, srcw in ((wk_sb, self.WkT), (wv_sb, self.WvT),
                                     (wr_sb, self.WrT), (wo_sb, self.WoT)):
                    gpsimd.dma_start(
                        out=dst_sb[:].rearrange("p (kt m) -> p kt m", kt=KD),
                        in_=srcw.rearrange("(kt p) m -> p kt m", p=128),
                    ).then_inc(wl.h, 16)
            wl.bump(64)
            self.sec(gpsimd=f_w)

            srcv = src.rearrange("(kt p) t -> p kt t", p=128)
            u2v = self.s_u2.rearrange("(kt p) t -> p kt t", p=128)
            xwv = xw[:].rearrange("p (k t) -> p k t", k=KD)
            u1v = xwv
            xmv = xm_b[:].rearrange("p (k t) -> p k t", k=KD)
            kv3 = k_sb[:].rearrange("p (k t) -> p k t", k=KD)
            vv3 = v_sb[:].rearrange("p (k t) -> p k t", k=KD)
            sr3 = sr_sb[:].rearrange("p (k t) -> p k t", k=KD)
            zb3 = zb[:].rearrange("p (k t) -> p k t", k=KD)

            ln_done = {}
            for c in range(NCH):
                t0 = c * TC

                def f_ld(sync, c=c, t0=t0, sst_t=sst.v):
                    if c >= 1:
                        sync.wait_ge(sst.h, sst_t)
                    lo = max(t0 - 1, 0)
                    hi = min(t0 + TC + 1, self.T)
                    off = 1 if c == 0 else 0
                    sync.dma_start(
                        out=xwv[:, :, off:off + (hi - lo)],
                        in_=srcv[:, :, lo:hi]).then_inc(xl.h, 16)
                v_x = xl.bump(16)
                self.sec(sync=f_ld)

                def f_gate(eng, v=v_x, c=c, smx_t=smx.v):
                    eng.wait_ge(xl.h, v)
                    eng.wait_ge(wl.h, wl.v)
                    if c >= 1:
                        eng.wait_ge(smx.h, smx_t)
                self.sec(tensor=f_gate, vector=f_gate, scalar=f_gate)

                def f_car(vector, c=c):
                    if c == 0:
                        vector.memset(u1v[:, :, 0:1], 0.0)
                    else:
                        vector.tensor_copy(u1v[:, :, 0:1],
                                           car[:].unsqueeze(2))
                    vector.drain()
                self.sec(vector=f_car)

                ctr = lambda a: a[:, :, 1:1 + TC]
                v_u0 = self.emit_ln(lnb, ctr(xwv), TC, out_f32=ctr(u1v))
                v_u1 = self.emit_ln(lnb, ctr(u1v), TC, out_f32=ctr(u1v),
                                    g=self.ppr("ln1g"), b=self.ppr("ln1b"),
                                    wait_sv=v_u0)
                if c < NCH - 1:
                    hb = self._halo_bufs(lnb)
                    v_h0 = self.emit_ln(hb, xwv[:, :, TW - 1:TW], 1,
                                        out_f32=u1v[:, :, TW - 1:TW],
                                        wait_sv=v_u1)
                    v_u1 = self.emit_ln(hb, u1v[:, :, TW - 1:TW], 1,
                                        out_f32=u1v[:, :, TW - 1:TW],
                                        g=self.ppr("ln1g"), b=self.ppr("ln1b"),
                                        wait_sv=v_h0)
                else:
                    def f_hz(vector):
                        vector.memset(u1v[:, :, TW - 1:TW], 0.0)
                        vector.drain()
                    self.sec(vector=f_hz)

                def f_sc(vector, v_u1=v_u1):
                    vector.tensor_copy(car[:].unsqueeze(2), u1v[:, :, TC:TC + 1])
                    vector.drain()
                self.sec(vector=f_sc)

                def f_mx(vector, smv_base=smv.v):
                    sv = smv_base
                    cur = u1v[:, :, 1:1 + TC]
                    prv = u1v[:, :, 0:TC]
                    nxt = u1v[:, :, 2:2 + TC]
                    HB = KD // 2
                    bc = lambda n: self.ppr(n).unsqueeze(2).broadcast_to(
                        (128, KD, TC))
                    tb3 = tb1[:].rearrange("p (k t) -> p k t", k=KD)
                    vector.tensor_mul(tb3, cur, bc("tmix")).then_inc(smv.h, 1)
                    vector.tensor_mul(xmv, prv, bc("tmix1m")).then_inc(smv.h, 1)
                    sv += 2
                    vector.wait_ge(smv.h, sv)
                    vector.tensor_add(tb3, tb3, xmv).then_inc(smv.h, 1)
                    sv += 1
                    vector.wait_ge(smv.h, sv)
                    cmb_ = bc("cmix")
                    vector.tensor_mul(xmv[:, 0:HB, :], prv[:, 0:HB, :],
                                      cmb_[:, 0:HB, :]).then_inc(smv.h, 1)
                    vector.tensor_mul(xmv[:, HB:, :], nxt[:, HB:, :],
                                      cmb_[:, HB:, :]).then_inc(smv.h, 1)
                    sv += 2
                    vector.wait_ge(smv.h, sv)
                    vector.tensor_add(xmv, xmv, tb3).then_inc(smv.h, 1)
                    vector.drain().then_inc(smx.h, 1)
                smv.bump(6)
                v_mx = smx.bump()
                self.sec(vector=f_mx)

                spj_base = spj.v
                sev_base = sev.v

                def f_kvr(tensor, v_mx=v_mx, spj_base=spj_base,
                          sev_base=sev_base):
                    tensor.wait_ge(smx.h, v_mx)
                    g = 0
                    for wsb in (wk_sb, wv_sb, wr_sb):
                        wv_ = wsb[:].rearrange("p (kt m) -> p kt m", kt=KD)
                        for mt in range(KD):
                            if g >= 2:
                                tensor.wait_ge(sev.h, sev_base + g - 1)
                            for kt in range(KD):
                                mm = tensor.matmul(
                                    kvr_ps[:, g % 2, :],
                                    wv_[:, kt, mt * 128:(mt + 1) * 128],
                                    xmv[:, kt, :],
                                    start=(kt == 0), stop=(kt == KD - 1))
                            mm.then_inc(spj.h, 1)
                            g += 1
                spj.bump(3 * KD)

                sza_base = sza.v

                def f_ev_v(vector, spj_base=spj_base, sev_base=sev_base):
                    for mt in range(KD):
                        vector.wait_ge(spj.h, spj_base + mt + 1)
                        vector.tensor_scalar_min(kv3[:, mt, :],
                                                 kvr_ps[:, mt % 2, :],
                                                 K_CLAMP).then_inc(sev.h, 1)
                sev.bump(KD)

                def f_ev_s(scalar, spj_base=spj_base, sev_base=sev_base, c=c):
                    scalar.wait_ge(sev.h, sev_base + KD)
                    scalar.activation(k_sb[:], k_sb[:], AF.Exp)
                    scalar.drain()
                    for mt in range(KD):
                        g = KD + mt
                        scalar.wait_ge(spj.h, spj_base + g + 1)
                        scalar.copy(vv3[:, mt, :],
                                    kvr_ps[:, g % 2, :]).then_inc(sev.h, 1)
                    for mt in range(KD):
                        g = 2 * KD + mt
                        scalar.wait_ge(spj.h, spj_base + g + 1)
                        scalar.activation(sr3[:, mt, :], kvr_ps[:, g % 2, :],
                                          AF.Sigmoid).then_inc(sev.h, 1)
                    scalar.drain().then_inc(sev.h, 1)
                sev.bump(2 * KD + 1)
                self.sec(tensor=f_kvr, vector=f_ev_v, scalar=f_ev_s)

                def f_wkv(vector, sev_t=sev.v, c=c):
                    vector.wait_ge(sev.h, sev_t)
                    for ct in range(KD):
                        edb = self.ppc("ed", ct).broadcast_to((128, TC))
                        vector.tensor_mul(kv_t[:], kv3[:, ct, :], vv3[:, ct, :])
                        vector.tensor_copy(ab_buf[:, 0:1], self.wkvA[:, ct:ct + 1])
                        vector.tensor_copy(bb_buf[:, 0:1], self.wkvB[:, ct:ct + 1])
                        vector.drain()
                        vector.tensor_tensor_scan(
                            ab_buf[:, 1:1 + TC], edb, kv_t[:],
                            self.wkvA[:, ct:ct + 1], ALU.mult, ALU.add)
                        vector.tensor_tensor_scan(
                            bb_buf[:, 1:1 + TC], edb, kv3[:, ct, :],
                            self.wkvB[:, ct:ct + 1], ALU.mult, ALU.add)
                        vector.drain()
                        vector.tensor_copy(self.wkvA[:, ct:ct + 1],
                                           ab_buf[:, TC:TC + 1])
                        vector.tensor_copy(self.wkvB[:, ct:ct + 1],
                                           bb_buf[:, TC:TC + 1])
                        vector.scalar_tensor_tensor(t1[:], kv_t[:],
                                                    self.ppc("eu", ct),
                                                    ab_buf[:, 0:TC],
                                                    ALU.mult, ALU.add)
                        vector.scalar_tensor_tensor(t2[:], kv3[:, ct, :],
                                                    self.ppc("eu", ct),
                                                    bb_buf[:, 0:TC],
                                                    ALU.mult, ALU.add)
                        vector.drain()
                        vector.reciprocal(t2[:], t2[:])
                        vector.drain()
                        vector.tensor_mul(t1[:], t1[:], t2[:])
                        vector.drain()
                        vector.tensor_mul(zb3[:, ct, :], sr3[:, ct, :],
                                          t1[:]).then_inc(swk.h, 1)
                swk.bump(KD)
                self.sec(vector=f_wkv)

                sao_base = sao.v
                su2_base = su2.v

                def f_ao(tensor, swk_t=swk.v, sao_base=sao_base,
                         su2_base=su2_base):
                    tensor.wait_ge(swk.h, swk_t)
                    wv_ = wo_sb[:].rearrange("p (kt m) -> p kt m", kt=KD)
                    for mt in range(KD):
                        if mt >= 2:
                            tensor.wait_ge(su2.h, su2_base + mt - 1)
                        for kt in range(KD):
                            mm = tensor.matmul(
                                ao_ps[:, mt % 2, :],
                                wv_[:, kt, mt * 128:(mt + 1) * 128],
                                zb3[:, kt, :],
                                start=(kt == 0), stop=(kt == KD - 1))
                        mm.then_inc(sao.h, 1)
                sao.bump(KD)

                def f_u2(vector, sao_base=sao_base, c=c):
                    for mt in range(KD):
                        vector.wait_ge(sao.h, sao_base + mt + 1)
                        vector.tensor_add(u1v[:, mt, 1:1 + TC],
                                          u1v[:, mt, 1:1 + TC],
                                          ao_ps[:, mt % 2, :]).then_inc(su2.h, 1)
                su2.bump(KD)
                self.sec(tensor=f_ao, vector=f_u2)

                def f_st(sync, t0=t0, su2_t=su2.v):
                    sync.wait_ge(su2.h, su2_t)
                    sync.dma_start(out=u2v[:, :, t0:t0 + TC],
                                   in_=u1v[:, :, 1:1 + TC]).then_inc(sst.h, 16)
                sst.bump(16)
                self.sec(sync=f_st)

            def f_wait(sync):
                sync.wait_ge(sst.h, sst.v)
            self.sec(sync=f_wait)

    def rwkv2(self, src, dst):
        self.begin_phase()
        NCH = self.NCH
        TW = TC + 2
        tg = "r2"
        with contextlib.ExitStack() as pst:
            sbp = lambda n, s, d=F32: self.sb(tg + n, s, d, pst)
            u3 = sbp("u3", [128, KD * TW])
            car = sbp("car", [128, KD])
            lnb = self.mk_ln_bufs(pst, tg + "l", TC)
            xm_b = sbp("xm", [128, KD * TC], BF16)
            fr_sb = sbp("fr", [128, KD * D], BF16)
            fk_s = [sbp(f"fk{i}", [128, KD * 512], BF16) for i in range(3)]
            fv_s = [sbp(f"fv{i}", [128, 4 * 512], BF16) for i in range(3)]
            kk_b = sbp("kk", [128, KF * TC], BF16)
            sf_sb = sbp("sf", [128, KD * TC], BF16)
            x2r = sbp("x2", [128, KD * TC])
            rt = [sbp(f"rt{i}", [128, TC], BF16) for i in range(2)]
            t1 = sbp("t1", [128, TC])
            tb1 = sbp("tb", [128, KD * TC], BF16)
            out_sb = sbp("o", [128, KD * TC])

            h_ps = self.ps(tg + "hp", [128, 2, TC], pst)
            y_ps = self.ps(tg + "yp", [128, 2, TC], pst)

            wl = self.gsem("sw")
            wk1 = [self.gsem("sw") for i in range(3)]
            wv1 = [self.gsem("sw") for i in range(3)]
            xl = self.gsem("hw")
            x2l = self.gsem("hw")
            smx = self.gsem("c")
            smv = self.gsem("c")
            shp = self.gsem("c")
            she = self.gsem("c")
            skk = self.gsem("c")
            syp = self.gsem("c")
            sfc = self.gsem("c")
            sfr = self.gsem("c")
            sfe = self.gsem("c")
            sou = self.gsem("c")
            sst = self.gsem("hw")
            shp_b = shp.v
            syp_b = syp.v
            sfc_b = sfc.v
            wk1_b = [s.v for s in wk1]
            wv1_b = [s.v for s in wv1]

            def f_w(gpsimd):
                gpsimd.wait_ge(self.ph.h, self.ph.v)
                gpsimd.dma_start(
                    out=fr_sb[:].rearrange("p (kt m) -> p kt m", kt=KD),
                    in_=self.FrT.rearrange("(kt p) m -> p kt m", p=128),
                ).then_inc(wl.h, 16)
                fkv = self.FkT.rearrange("(kt p) m -> p kt m", p=128)
                fvv = self.FvT.rearrange("(g kt p) m -> g p kt m", p=128, kt=4)
                for c in range(NCH):
                    for fb in range(KF // 4):
                        idx = c * (KF // 4) + fb
                        slot = idx % 3
                        if idx >= 3:
                            gpsimd.wait_ge(shp.h, shp_b + 4 * (idx - 2))
                        gpsimd.dma_start(
                            out=fk_s[slot][:].rearrange("p (kt m) -> p kt m", kt=KD),
                            in_=fkv[:, :, fb * 512:(fb + 1) * 512],
                        ).then_inc(wk1[slot].h, 16)
                        wk1[slot].bump(16)
                    for half in range(2):
                        for pair in range(2):
                            for g in range(8):
                                idx = ((c * 2 + half) * 2 + pair) * 8 + g
                                slot = idx % 3
                                if idx >= 3:
                                    pidx = idx - 3
                                    if pidx % 8 == 7:
                                        gpsimd.wait_ge(syp.h, syp_b +
                                                       pidx // 8 + 1)
                                    else:
                                        gpsimd.wait_ge(sfc.h, sfc_b +
                                                       pidx + 1 - pidx // 8)
                                gpsimd.dma_start(
                                    out=fv_s[slot][:].rearrange(
                                        "p (kt m) -> p kt m", kt=4),
                                    in_=fvv[g, :, :, half * 512:(half + 1) * 512],
                                ).then_inc(wv1[slot].h, 16)
                                wv1[slot].bump(16)
            wl.bump(16)
            for i in range(3):
                pass
            self.sec(gpsimd=f_w)

            srcv = src.rearrange("(kt p) t -> p kt t", p=128)
            dstv = dst.rearrange("(kt p) t -> p kt t", p=128)
            u2v = self.s_u2.rearrange("(kt p) t -> p kt t", p=128)
            u3v = u3[:].rearrange("p (k t) -> p k t", k=KD)
            xmv = xm_b[:].rearrange("p (k t) -> p k t", k=KD)
            kk3 = kk_b[:].rearrange("p (k t) -> p k t", k=KF)
            sf3 = sf_sb[:].rearrange("p (k t) -> p k t", k=KD)
            x2r3 = x2r[:].rearrange("p (k t) -> p k t", k=KD)
            out3 = out_sb[:].rearrange("p (k t) -> p k t", k=KD)

            nfk = [0, 0, 0]
            nfv = [0, 0, 0]
            for c in range(NCH):
                t0 = c * TC

                def f_ld(sync, c=c, t0=t0, sst_t=sst.v):
                    if c >= 1:
                        sync.wait_ge(sst.h, sst_t)
                    lo = max(t0 - 1, 0)
                    hi = min(t0 + TC + 1, self.T)
                    off = 1 if c == 0 else 0
                    sync.dma_start(
                        out=u3v[:, :, off:off + (hi - lo)],
                        in_=u2v[:, :, lo:hi]).then_inc(xl.h, 16)
                    sync.dma_start(out=x2r3, in_=srcv[:, :, t0:t0 + TC]
                                   ).then_inc(x2l.h, 16)
                v_x = xl.bump(16)
                v_x2 = x2l.bump(16)
                self.sec(sync=f_ld)

                def f_gate(eng, v=v_x, c=c, smx_t=smx.v):
                    eng.wait_ge(xl.h, v)
                    eng.wait_ge(wl.h, wl.v)
                    if c >= 1:
                        eng.wait_ge(smx.h, smx_t)
                self.sec(tensor=f_gate, vector=f_gate, scalar=f_gate)

                def f_car(vector, c=c):
                    if c == 0:
                        vector.memset(u3v[:, :, 0:1], 0.0)
                    else:
                        vector.tensor_copy(u3v[:, :, 0:1], car[:].unsqueeze(2))
                    vector.drain()
                self.sec(vector=f_car)

                ctr = lambda a: a[:, :, 1:1 + TC]
                v_u3 = self.emit_ln(lnb, ctr(u3v), TC, out_f32=ctr(u3v),
                                    g=self.ppr("ln2g"), b=self.ppr("ln2b"))
                if c < NCH - 1:
                    hb = self._halo_bufs(lnb)
                    v_u3 = self.emit_ln(hb, u3v[:, :, TW - 1:TW], 1,
                                        out_f32=u3v[:, :, TW - 1:TW],
                                        g=self.ppr("ln2g"), b=self.ppr("ln2b"),
                                        wait_sv=v_u3)
                else:
                    def f_hz(vector):
                        vector.memset(u3v[:, :, TW - 1:TW], 0.0)
                        vector.drain()
                    self.sec(vector=f_hz)

                def f_sc(vector, v_u3=v_u3):
                    vector.tensor_copy(car[:].unsqueeze(2), u3v[:, :, TC:TC + 1])
                    vector.drain()
                self.sec(vector=f_sc)

                def f_mx(vector, smv_base=smv.v):
                    sv = smv_base
                    cur = u3v[:, :, 1:1 + TC]
                    prv = u3v[:, :, 0:TC]
                    nxt = u3v[:, :, 2:2 + TC]
                    HB = KD // 2
                    bc = lambda n: self.ppr(n).unsqueeze(2).broadcast_to(
                        (128, KD, TC))
                    tb3 = tb1[:].rearrange("p (k t) -> p k t", k=KD)
                    vector.tensor_mul(tb3, cur, bc("ftmix")).then_inc(smv.h, 1)
                    vector.tensor_mul(xmv, prv, bc("ftmix1m")).then_inc(smv.h, 1)
                    sv += 2
                    vector.wait_ge(smv.h, sv)
                    vector.tensor_add(tb3, tb3, xmv).then_inc(smv.h, 1)
                    sv += 1
                    vector.wait_ge(smv.h, sv)
                    cmb_ = bc("fcmix")
                    vector.tensor_mul(xmv[:, 0:HB, :], prv[:, 0:HB, :],
                                      cmb_[:, 0:HB, :]).then_inc(smv.h, 1)
                    vector.tensor_mul(xmv[:, HB:, :], nxt[:, HB:, :],
                                      cmb_[:, HB:, :]).then_inc(smv.h, 1)
                    sv += 2
                    vector.wait_ge(smv.h, sv)
                    vector.tensor_add(xmv, xmv, tb3).then_inc(smv.h, 1)
                    vector.drain().then_inc(smx.h, 1)
                smv.bump(6)
                v_mx = smx.bump()
                self.sec(vector=f_mx)

                sfr_base = sfr.v
                sfe_base = sfe.v

                def f_fr(tensor, v_mx=v_mx, sfr_base=sfr_base,
                         sfe_base=sfe_base, c=c, she_t=she.v):
                    tensor.wait_ge(smx.h, v_mx)
                    if c >= 1:
                        tensor.wait_ge(she.h, she_t)
                    wv_ = fr_sb[:].rearrange("p (kt m) -> p kt m", kt=KD)
                    for mt in range(KD):
                        if mt >= 2:
                            tensor.wait_ge(sfe.h, sfe_base + mt - 1)
                        for kt in range(KD):
                            mm = tensor.matmul(
                                h_ps[:, mt % 2, :],
                                wv_[:, kt, mt * 128:(mt + 1) * 128],
                                xmv[:, kt, :],
                                start=(kt == 0), stop=(kt == KD - 1))
                        mm.then_inc(sfr.h, 1)
                sfr.bump(KD)

                def f_fre(scalar, sfr_base=sfr_base):
                    for mt in range(KD):
                        scalar.wait_ge(sfr.h, sfr_base + mt + 1)
                        scalar.activation(sf3[:, mt, :], h_ps[:, mt % 2, :],
                                          AF.Sigmoid)
                        scalar.drain().then_inc(sfe.h, 1)
                sfe.bump(KD)
                self.sec(tensor=f_fr, scalar=f_fre)

                shp_base = shp.v
                she_base = she.v
                skk_base = skk.v

                def f_fk(tensor, c=c, v_mx=v_mx, shp_base=shp_base,
                         she_base=she_base, sfe_t=sfe.v):
                    tensor.wait_ge(sfe.h, sfe_t)
                    for fb in range(KF // 4):
                        idx = c * (KF // 4) + fb
                        slot = idx % 3
                        nfk[slot] += 1
                        tensor.wait_ge(wk1[slot].h, wk1_b[slot] + nfk[slot] * 16)
                        wv_ = fk_s[slot][:].rearrange("p (kt m) -> p kt m", kt=KD)
                        for ft in range(4):
                            gg = fb * 4 + ft
                            if gg >= 2:
                                tensor.wait_ge(she.h, she_base + gg - 1)
                            for kt in range(KD):
                                mm = tensor.matmul(
                                    h_ps[:, gg % 2, :],
                                    wv_[:, kt, ft * 128:(ft + 1) * 128],
                                    xmv[:, kt, :],
                                    start=(kt == 0), stop=(kt == KD - 1))
                            mm.then_inc(shp.h, 1)
                shp.bump(KF)

                def f_fke(scalar, shp_base=shp_base, skk_base=skk_base):
                    for gg in range(KF):
                        scalar.wait_ge(shp.h, shp_base + gg + 1)
                        scalar.activation(rt[gg % 2][:], h_ps[:, gg % 2, :],
                                          AF.Relu).then_inc(she.h, 1)
                        scalar.drain()
                        scalar.activation(kk3[:, gg, :], rt[gg % 2][:],
                                          AF.Square).then_inc(skk.h, 1)
                she.bump(KF)
                skk.bump(KF)
                self.sec(tensor=f_fk, scalar=f_fke)

                syp_base = syp.v
                sou_base = sou.v

                def f_fv(tensor, c=c, skk_t=skk.v, syp_base=syp_base,
                         sou_base=sou_base):
                    tensor.wait_ge(skk.h, skk_t)
                    for half in range(2):
                        for pair in range(2):
                            pp_ = (c * 2 + half) * 2 + pair
                            if half * 2 + pair >= 1:
                                tensor.wait_ge(sou.h,
                                               sou_base + (half * 2 + pair) * 2)
                            elif c >= 1:
                                tensor.wait_ge(sou.h, sou_base)
                            for g in range(8):
                                idx = pp_ * 8 + g
                                slot = idx % 3
                                nfv[slot] += 1
                                tensor.wait_ge(wv1[slot].h, wv1_b[slot] + nfv[slot] * 16)
                                wv_ = fv_s[slot][:].rearrange(
                                    "p (kt m) -> p kt m", kt=4)
                                for kt in range(4):
                                    gkt = g * 4 + kt
                                    for mi2 in range(2):
                                        mloc = pair * 2 + mi2
                                        mm = tensor.matmul(
                                            y_ps[:, mi2, :],
                                            wv_[:, kt, mloc * 128:(mloc + 1) * 128],
                                            kk3[:, gkt, :],
                                            start=(gkt == 0),
                                            stop=(gkt == KF - 1))
                                    if gkt == KF - 1:
                                        mm.then_inc(syp.h, 1)
                                if g < 7:
                                    mm.then_inc(sfc.h, 1)
                syp.bump(4)
                sfc.bump(28)

                def f_ou(vector, c=c, syp_base=syp_base, sfe_t=sfe.v,
                         v_x2=v_x2, sou_base=sou_base, smv_base=smv.v):
                    vector.wait_ge(sfe.h, sfe_t)
                    vector.wait_ge(x2l.h, v_x2)
                    tb3o = tb1[:].rearrange("p (k t) -> p k t", k=KD)
                    for mt in range(KD):
                        vector.wait_ge(syp.h, syp_base + mt // 2 + 1)
                        vector.tensor_mul(tb3o[:, mt, :], y_ps[:, mt % 2, :],
                                          sf3[:, mt, :]).then_inc(sou.h, 1)
                    vector.wait_ge(sou.h, sou_base + KD)
                    vector.tensor_add(out3, tb3o,
                                      u3v[:, :, 1:1 + TC]).then_inc(smv.h, 1)
                    vector.wait_ge(smv.h, smv_base + 1)
                    vector.tensor_add(out3, out3,
                                      x2r3).then_inc(smv.h, 1)
                sou.bump(KD)
                smv.bump(2)
                self.sec(tensor=f_fv, vector=f_ou)

                def f_st(sync, t0=t0, sou_t=sou.v, smv_t=smv.v):
                    sync.wait_ge(sou.h, sou_t)
                    sync.wait_ge(smv.h, smv_t)
                    sync.dma_start(out=dstv[:, :, t0:t0 + TC], in_=out3
                                   ).then_inc(sst.h, 16)
                sst.bump(16)
                self.sec(sync=f_st)

            def f_wait(sync):
                sync.wait_ge(sst.h, sst.v)
            self.sec(sync=f_wait)



def _pp(v, k):
    return np.ascontiguousarray(np.asarray(v, np.float32).reshape(k, 128).T)


def prep_shared(i):
    import ml_dtypes
    f32 = np.float32

    def t(a):
        return np.ascontiguousarray(np.asarray(a, f32).T)

    out = {
        "xpT": t(i["L0_attn_a_x_proj_w"]),
        "dtpT": t(i["L0_attn_a_dt_proj_w"]),
        "bcT": np.ascontiguousarray(np.concatenate(
            [np.asarray(i["L0_attn_a_B_proj_w"], f32),
             np.asarray(i["L0_attn_a_C_proj_w"], f32)], 0).T),
        "opT": t(i["L0_attn_a_out_proj_w"]),
        "A_log_pp": np.ascontiguousarray(
            np.asarray(i["L0_attn_a_A_log"], f32).reshape(KI, 128, NS)
            .transpose(1, 0, 2).reshape(128, KI * NS)),
        "dtb_pp": _pp(i["L0_attn_a_dt_proj_b"], KI),
        "convw_pp": np.ascontiguousarray(
            np.asarray(i["L0_attn_a_conv_w"], f32).reshape(DI, 4)
            .reshape(KI, 128, 4).transpose(1, 0, 2).reshape(128, KI * 4)),
        "convb_pp": _pp(i["L0_attn_a_conv_b"], KI),
        "D_pp": _pp(i["L0_attn_a_D"], KI),
        "WkT": t(i["L0_attn_b_Wk"]), "WvT": t(i["L0_attn_b_Wv"]),
        "WrT": t(i["L0_attn_b_Wr"]), "WoT": t(i["L0_attn_b_Wo"]),
        "FkT": t(i["L0_attn_b_Fk"]), "FvT": t(i["L0_attn_b_Fv"]),
        "FrT": t(i["L0_attn_b_Fr"]),
        "ln1g_pp": _pp(i["L0_attn_b_ln1_g"], KD),
        "ln1b_pp": _pp(i["L0_attn_b_ln1_b"], KD),
        "ln2g_pp": _pp(i["L0_attn_b_ln2_g"], KD),
        "ln2b_pp": _pp(i["L0_attn_b_ln2_b"], KD),
        "tmix_pp": _pp(np.asarray(i["L0_attn_b_att_time_mix"]).reshape(D), KD),
        "cmix_pp": _pp(np.asarray(i["L0_attn_b_att_combined_mix"]).reshape(D), KD),
        "ftmix_pp": _pp(np.asarray(i["L0_attn_b_ffn_time_mix"]).reshape(D), KD),
        "fcmix_pp": _pp(np.asarray(i["L0_attn_b_ffn_combined_mix"]).reshape(D), KD),
        "tdec_pp": _pp(i["L0_attn_b_time_decay"], KD),
        "tfir_pp": _pp(i["L0_attn_b_time_first"], KD),
        "mixw": np.ascontiguousarray(np.asarray(i["L0_ffn_mix_w"], f32)),
        "mixb_pp": _pp(i["L0_ffn_mix_b"], KD),
        "g1_pp": _pp(i["L0_ffn_g1"], KD), "be1_pp": _pp(i["L0_ffn_be1"], KD),
        "g2_pp": _pp(i["L0_ffn_g2"], KD), "be2_pp": _pp(i["L0_ffn_be2"], KD),
        "routerw": np.ascontiguousarray(np.asarray(i["L0_ffn_router_w"], f32)),
        "w1": np.ascontiguousarray(np.asarray(i["L0_ffn_w1"], f32)),
        "b1_pp": np.ascontiguousarray(
            np.asarray(i["L0_ffn_b1"], f32).reshape(NE, KF, 128)
            .transpose(2, 0, 1).reshape(128, NE * KF)),
        "w2": np.ascontiguousarray(np.asarray(i["L0_ffn_w2"], f32)),
        "b2": np.ascontiguousarray(np.asarray(i["L0_ffn_b2"], f32)),
        "eye32": np.eye(128, dtype=f32),
        "ones1": np.ones((1, 128), f32),
        "ones1b": np.ones((128, 1), ml_dtypes.bfloat16),
    }
    sel16 = np.zeros((NS, NS * 128), ml_dtypes.bfloat16)
    for s in range(NS):
        sel16[s, s * 128:(s + 1) * 128] = 1.0
    out["sel16"] = sel16
    sel8 = np.zeros((NE, NE * 128), f32)
    for e in range(NE):
        sel8[e, e * 128:(e + 1) * 128] = 1.0
    out["sel8"] = sel8
    return out


def kernel(**inputs):
    x = np.asarray(inputs["L0_attn_a_x"], np.float32)
    B, T, _ = x.shape
    shared = prep_shared(inputs)
    nc = KB(T=T).build()
    in_maps = []
    for b in range(B):
        m = dict(shared)
        m["xT"] = np.ascontiguousarray(x[b].T)
        in_maps.append(m)
    res = run_bass_kernel_spmd(nc, in_maps, core_ids=list(range(B)))
    return np.stack([np.ascontiguousarray(r["yout"].T) for r in res.results], 0)
